# version 17
# baseline (speedup 1.0000x reference)
import numpy as np
import ml_dtypes
import concourse.bass as bass
import concourse.mybir as mybir
from concourse.bass_utils import run_bass_kernel_spmd

F32 = mybir.dt.float32
BF16 = mybir.dt.bfloat16
I32 = mybir.dt.int32
AF = mybir.ActivationFunctionType
ALU = mybir.AluOpType
bfdt = ml_dtypes.bfloat16

D = 2048
T = 2048
NH = 16
HD = 128
G = 16
HID = 5632
NHB = HID // 128
WIN = 256
NT = T // 128
ND = D // 128
EPS = 1e-5
QSCALE = float(HD) ** -0.5
NEG = -30000.0


class Sem:
    def __init__(self, nc, name):
        self.g = nc.semaphore(name)
        self.h = self.g.__enter__()
        self.n = 0

    def inc(self, instr, k=1):
        instr.then_inc(self.h, k)
        self.n += k
        return (self, self.n)

    @property
    def tok(self):
        return (self, self.n)


def _wait_only(act, f):
    f(act)
    return None


def wt(eng, token):
    if token is None:
        return
    sem, n = token
    if n > 0:
        eng.wait_ge(sem.h, n)


class Bank:

    def __init__(self, psum):
        self.p = psum
        self.freed = None
        self.used = None


def build_nc(debug=False):
    nc = bass.Bass("TRN2", target_bir_lowering=False)

    x_in = nc.dram_tensor("x", [T, D], F32, kind="ExternalInput").ap()
    g_in = nc.dram_tensor("g", [G, D], F32, kind="ExternalInput").ap()
    lnw_in = nc.dram_tensor("lnw", [128, ND], F32, kind="ExternalInput").ap()
    lnb_in = nc.dram_tensor("lnb", [128, ND], F32, kind="ExternalInput").ap()
    wq_in = nc.dram_tensor("wq", [NH, 128, D], F32, kind="ExternalInput").ap()
    wk_in = nc.dram_tensor("wk", [NH, 128, D], F32, kind="ExternalInput").ap()
    wv_in = nc.dram_tensor("wv", [4, 128, ND * 512], F32, kind="ExternalInput").ap()
    wo_in = nc.dram_tensor("wo", [4, 128, ND * 512], F32, kind="ExternalInput").ap()
    wg_in = nc.dram_tensor("wg", [NHB, 128, D], I32, kind="ExternalInput").ap()
    wu_in = nc.dram_tensor("wu", [NHB, 128, D], I32, kind="ExternalInput").ap()
    wd_in = nc.dram_tensor("wd", [4, HID, 512], I32, kind="ExternalInput").ap()
    sg_in = nc.dram_tensor("sgv", [128, NHB], F32, kind="ExternalInput").ap()
    su_in = nc.dram_tensor("suv", [128, NHB], F32, kind="ExternalInput").ap()
    sd_in = nc.dram_tensor("sdv", [1, D], F32, kind="ExternalInput").ap()
    id_in = nc.dram_tensor("ident", [128, 128], BF16, kind="ExternalInput").ap()
    mk_in = nc.dram_tensor("mask6", [128, 6 * 512], BF16, kind="ExternalInput").ap()
    oc_in = nc.dram_tensor("ones_col", [128, 1], BF16, kind="ExternalInput").ap()
    or_in = nc.dram_tensor("ones_row", [1, 128], BF16, kind="ExternalInput").ap()
    n128_in = nc.dram_tensor("neg128", [1, 512], BF16, kind="ExternalInput").ap()

    out_dram = nc.dram_tensor("out", [T, D], F32, kind="ExternalOutput").ap()
    scratch_kind = "ExternalOutput" if debug else "Internal"
    r2_dram = nc.dram_tensor("r2sp", [T, D], F32, kind=scratch_kind).ap()
    ao_dram = nc.dram_tensor("aosp", [NT, NH, 128, 128], BF16, kind=scratch_kind).ap()
    v2_dram = nc.dram_tensor("v2sp", [2, 4, 8, 128, 512], F32, kind="Internal").ap()
    xn_tap = None
    if debug:
        xn_tap = nc.dram_tensor("xnfmd", [128, ND * T], BF16, kind="ExternalOutput").ap()

    s_pe = Sem(nc, "s_pe")
    s_dve = Sem(nc, "s_dve")
    s_act = Sem(nc, "s_act")
    s_c0 = Sem(nc, "s_c0")
    s_cg = Sem(nc, "s_cg")
    s_cgx = Sem(nc, "s_cgx")
    s_x = [Sem(nc, "s_x0"), Sem(nc, "s_x1")]
    s_xb1 = [Sem(nc, "s_xb1a"), Sem(nc, "s_xb1b")]
    s_wqk = Sem(nc, "s_wqk")
    s_wv = Sem(nc, "s_wv")
    s_ao = [Sem(nc, "s_ao0"), Sem(nc, "s_ao1")]
    s_wo = [Sem(nc, "s_wo0"), Sem(nc, "s_wo1")]
    s_aot = [Sem(nc, "s_aot0"), Sem(nc, "s_aot1")]
    s_xr = [Sem(nc, "s_xr0"), Sem(nc, "s_xr1")]
    s_r2s = Sem(nc, "s_r2s")
    s_r2l = [Sem(nc, "s_r2l0"), Sem(nc, "s_r2l1")]
    s_n2l = [Sem(nc, "s_n2l%d" % i) for i in range(4)]
    s_tr = [Sem(nc, "s_tr%d" % i) for i in range(8)]
    s_wos = [Sem(nc, "s_wos%d" % i) for i in range(4)]
    s_xb2 = [Sem(nc, "s_xb2a"), Sem(nc, "s_xb2b")]
    s_wg = [Sem(nc, "s_wg0"), Sem(nc, "s_wg1")]
    s_wd = [Sem(nc, "s_wd0"), Sem(nc, "s_wd1")]
    s_v2s = [Sem(nc, "s_v2s0"), Sem(nc, "s_v2s1")]
    s_v2l = Sem(nc, "s_v2l")
    s_out = Sem(nc, "s_out")
    all_sems = [s_pe, s_dve, s_act, s_c0, s_cg, s_cgx, s_wqk, s_wv, s_r2s,
                s_v2l, s_out] + s_x + s_xb1 + s_ao + s_wo + s_aot + s_xr +         s_r2l + s_n2l + s_tr + s_wos + s_xb2 + s_wg + s_wd + s_v2s

    from contextlib import ExitStack
    outer = ExitStack()

    def sba(stk, name, shape, dt, **kw):
        return stk.enter_context(nc.sbuf_tensor(name, shape, dt, **kw))

    def psa(stk, name, shape):
        return stk.enter_context(nc.psum_tensor(name, shape, F32))

    def es_psum_bf16(nc_, stk, name):
        return stk.enter_context(nc_.psum_tensor(name, [128, 128], BF16))

    ones_col = sba(outer, "ones_colt", [128, 1], BF16)
    ones_row = sba(outer, "ones_rowt", [1, 128], BF16)
    neg128 = sba(outer, "neg128t", [1, 512], BF16)
    lnw = sba(outer, "lnwt", [128, ND], F32)
    lnb = sba(outer, "lnbt", [128, ND], F32)
    sgt = sba(outer, "sgtt", [128, NHB], F32)
    sut = sba(outer, "sutt", [128, NHB], F32)
    stats = sba(outer, "statst", [128, 24], F32)
    agm = sba(outer, "agt", [128, 2], F32)
    rstd = sba(outer, "rstdt", [128, 1], F32)
    sqv = sba(outer, "sqvt", [128, 1], F32)
    rowA = sba(outer, "rowAt", [1, T], F32)
    rowB = sba(outer, "rowBt", [1, T], F32)
    rowC = sba(outer, "rowCt", [1, T], F32)
    rowH = sba(outer, "rowHt", [1, T], BF16)
    rowL = sba(outer, "rowLt", [1, T], BF16)
    rb_sb = sba(outer, "rb_sbt", [128, 512], F32)
    ident = sba(outer, "identt", [128, 128], BF16)
    epsv = sba(outer, "epsvt", [128, 1], F32)
    sd_bc = sba(outer, "sd_bct", [128, D], F32)

    act_q = []

    def da_emit(fn, inc=1):
        s_act.n += inc
        act_q.append((fn, s_act.n, inc))
        return (s_act, s_act.n)

    with nc.Block() as blk:

        def flush_act():
            q = list(act_q)
            act_q.clear()
            if not q:
                return

            def body(act):
                for fn, n_after, inc in q:
                    ins = fn(act)
                    if inc:
                        ins.then_inc(s_act.h, inc)
            blk.scalar(body)

        def barrier():
            toks = [(s.h, s.n) for s in all_sems]

            def f(eng):
                for h, n in toks:
                    if n > 0:
                        eng.wait_ge(h, n)
            blk.tensor(f)
            blk.vector(f)
            da_emit(lambda act, f=f: f(act) or act.engine_nop() if False else _wait_only(act, f), 0)
            blk.sync(f)
            blk.gpsimd(f)

        p2s = ExitStack()
        masks = sba(p2s, "maskt", [128, 6 * 512], BF16)
        g_bf = sba(p2s, "g_bft", [G, D], BF16)
        g_fmc = sba(p2s, "g_fmct", [128, G * ND], BF16)
        gtr_ev = {}
        sd_row = sba(p2s, "sd_rowt", [1, D], F32)

        def f(sync):
            for dst, src in [(ident, id_in), (masks, mk_in), (ones_col, oc_in),
                             (ones_row, or_in), (neg128, n128_in), (lnw, lnw_in),
                             (lnb, lnb_in), (sgt, sg_in), (sut, su_in),
                             (sd_row, sd_in)]:
                s_c0.inc(sync.dma_start(out=dst[:, :], in_=src[:, :]), 16)
        blk.sync(f)
        t_consts = s_c0.tok

        def f(pool):
            s_cg.inc(pool.dma_start(out=g_bf[:, :], in_=g_in[:, :]), 16)
        blk.gpsimd(f)
        t_gbf = s_cg.tok

        def f(dve):
            s_dve.inc(dve.memset(epsv[:, :], EPS))
        blk.vector(f)

        with nc.psum_tensor("p_gtr", [128, 128], BF16) as p_gtr:
            for c in range(ND):
                def f(t, c=c):
                    if c == 0:
                        wt(t, t_gbf)
                        wt(t, t_consts)
                    else:
                        wt(t, (s_dve, gtr_ev[c - 1]))
                    ins = t.transpose(p_gtr[0:128, 0:16], g_bf[:, c * 128:(c + 1) * 128],
                                      ident[0:16, 0:16])
                    s_pe.inc(ins)
                blk.tensor(f)
                t_m = s_pe.tok

                def f(dve, c=c, t_m=t_m):
                    wt(dve, t_m)
                    dve.wait_ge(s_dve.h, s_dve.n)
                    s_dve.inc(dve.tensor_copy(g_fmc[:, c * G:(c + 1) * G], p_gtr[0:128, 0:16]))
                blk.vector(f)
                gtr_ev[c] = s_dve.n
        t_gfmc = s_dve.tok

        def f(dve):
            wt(dve, t_consts)
            s_dve.inc(dve.tensor_copy(rowH[:, 0:D], sd_row[:, :]))
            dve.wait_ge(s_dve.h, s_dve.n)
            s_dve.inc(dve.tensor_tensor(rowC[:, 0:D], sd_row[:, :], rowH[:, 0:D], ALU.subtract))
            dve.wait_ge(s_dve.h, s_dve.n)
            s_dve.inc(dve.tensor_copy(rowL[:, 0:D], rowC[:, 0:D]))
        blk.vector(f)
        t_sdhl = s_dve.tok

        with nc.psum_tensor("p_sdb", [128, 512], F32) as p_sdb:
            sdb_toks = []
            for j in range(4):
                def f(t, j=j):
                    wt(t, t_sdhl)
                    if sdb_toks:
                        wt(t, sdb_toks[-1])
                    t.matmul(p_sdb[:, :], ones_row[:, :], rowH[:, j * 512:(j + 1) * 512],
                             start=True, stop=False)
                    ins = t.matmul(p_sdb[:, :], ones_row[:, :], rowL[:, j * 512:(j + 1) * 512],
                                   start=False, stop=True)
                    s_pe.inc(ins)
                blk.tensor(f)
                t_mm = s_pe.tok

                def f(dve, j=j, t_mm=t_mm):
                    wt(dve, t_mm)
                    dve.wait_ge(s_dve.h, s_dve.n)
                    s_dve.inc(dve.tensor_copy(sd_bc[:, j * 512:(j + 1) * 512], p_sdb[:, :]))
                blk.vector(f)
                sdb_toks.append(s_dve.tok)

            pass

        p1s = ExitStack()
        xnfm = sba(p1s, "xnfmt", [128, ND * T], BF16)
        p1i = ExitStack()
        xtile = sba(p1i, "xtilet", [128, 2 * D], F32)
        n1 = sba(p1i, "n1t", [128, 2 * D], BF16)
        p1p = ExitStack()
        trb = [Bank(es_psum_bf16(nc, p1p, "p_tr0")), Bank(es_psum_bf16(nc, p1p, "p_tr1"))]

        n1_tok = {}
        xb_tok = {}
        ln_act_tok = {"prev": None}
        ln_dve_tok = {"prev": None}

        def emit_ln_stats(get_v, tok_ready):
            def f(dve):
                wt(dve, tok_ready)
                wt(dve, ln_act_tok["prev"])
                dve.wait_ge(s_dve.h, s_dve.n)
                vv = get_v()
                for i in range(4):
                    ins = dve.bn_stats(stats[:, i * 6:(i + 1) * 6], vv[:, i * 512:(i + 1) * 512])
                s_dve.inc(ins)
                dve.wait_ge(s_dve.h, s_dve.n)
                s_dve.inc(dve.bn_aggr(agm[:, :], stats[:, :]))
            blk.vector(f)
            t_ag = s_dve.tok

            def f(act, t_ag=t_ag, t_prev=ln_dve_tok["prev"]):
                wt(act, t_ag)
                wt(act, t_prev)
                return act.activation(sqv[:, :], agm[:, 1:2], AF.Sqrt, bias=epsv[:, 0:1])
            t_sq = da_emit(f)
            ln_act_tok["prev"] = t_sq

            def f(dve):
                wt(dve, t_sq)
                dve.wait_ge(s_dve.h, s_dve.n)
                s_dve.inc(dve.reciprocal(rstd[:, :], sqv[:, :]))
            blk.vector(f)
            ln_dve_tok["prev"] = s_dve.tok
            return s_dve.tok

        xt_tok = {}
        for tt in range(NT):
            buf = tt % 2
            hstep = tt % 8

            def f(sync, tt=tt, buf=buf):
                if tt >= 2:
                    wt(sync, n1_tok[tt - 2])
                s_x[buf].inc(sync.dma_start(out=xtile[:, buf * D:(buf + 1) * D],
                                            in_=x_in[tt * 128:(tt + 1) * 128, :]), 16)
            blk.sync(f)
            xt_tok[tt] = s_x[buf].tok

            t_rstd = emit_ln_stats(lambda buf=buf: xtile[:, buf * D:(buf + 1) * D], xt_tok[tt])

            def f(dve, tt=tt, buf=buf, t_rstd=t_rstd):
                wt(dve, t_rstd)
                if tt >= 2:
                    wt(dve, xb_tok[tt - 2])
                s_dve.inc(dve.tensor_scalar(n1[:, buf * D:(buf + 1) * D],
                                            xtile[:, buf * D:(buf + 1) * D],
                                            agm[:, 0:1], rstd[:, 0:1],
                                            ALU.subtract, ALU.mult))
            blk.vector(f)
            n1_tok[tt] = s_dve.tok

            for c in range(ND):
                bank = trb[c % 2]

                def f(t, tt=tt, buf=buf, c=c, bank=bank):
                    if c == 0:
                        wt(t, n1_tok[tt])
                        if tt == 0:
                            wt(t, t_consts)
                    wt(t, bank.freed)
                    ins = t.transpose(bank.p[:, 0:128],
                                      n1[:, buf * D + c * 128:buf * D + (c + 1) * 128],
                                      ident[:, :])
                    bank.used = s_pe.inc(ins)
                blk.tensor(f)

                def f(dve, tt=tt, c=c, bank=bank):
                    wt(dve, bank.used)
                    dve.wait_ge(s_dve.h, s_dve.n)
                    ins = dve.tensor_scalar(xnfm[:, c * T + tt * 128:c * T + (tt + 1) * 128],
                                            bank.p[:, 0:128], lnw[:, c:c + 1], lnb[:, c:c + 1],
                                            ALU.mult, ALU.add)
                    bank.freed = s_dve.inc(ins)
                blk.vector(f)
            xb_tok[tt] = s_pe.tok
        t_xnfm = s_dve.tok

        if debug:
            def f(sync):
                wt(sync, t_xnfm)
                s_out.inc(sync.dma_start(out=xn_tap[:, :], in_=xnfm[:, :]), 16)
            blk.sync(f)

        p1p.close()
        p1i.close()
        barrier()

        p2i = ExitStack()
        qb = sba(p2i, "qbt", [128, T], BF16)
        kb = sba(p2i, "kbt", [128, G + T], BF16)
        v4 = sba(p2i, "v4t", [128, 17 * 512], BF16)
        wqkT = sba(p2i, "wqkTt", [128, 2 * D], BF16)
        wvT = sba(p2i, "wvTt", [128, ND * 512], BF16)
        p_sb = sba(p2i, "p_sbt", [128, 4 * 512], BF16)
        p_g = sba(p2i, "p_gt", [16, 512], BF16)
        aostg = sba(p2i, "aostgt", [128, 2 * 512], BF16)

        pp = ExitStack()
        b_sc = [Bank(psa(pp, f"p_sc{i}", [128, 512])) for i in range(4)]
        b_pv = [Bank(psa(pp, f"p_pv{i}", [128, 512])) for i in range(2)]
        b_sr = Bank(psa(pp, "p_srow", [128, 512]))
        b_rb = Bank(psa(pp, "p_rbc", [128, 512]))

        emit_pv.blk = blk
        emit_pv.s_pe = s_pe
        emit_pv.b_pv = b_pv
        emit_pv.b_sr = b_sr
        emit_pv.v4 = v4
        emit_pv.p_sb = p_sb
        emit_pv.p_g = p_g
        emit_pv.ones_col = ones_col

        t_wqk = {}
        t_wv = {}
        t_qkprod = {}
        t_qk_evac = {}
        t_vprod = {}
        t_v_evac = {}
        t_lastscore = {}
        t_pv = {}
        t_ao = {}
        t_ao_st = {}
        t_rbfree = {}

        pending_tail = []

        def kts(q4):
            return list(range(max(0, 4 * q4 - 2), min(NT, 4 * q4 + 4)))

        sc_i = {"i": 0}

        def next_sc():
            b = b_sc[sc_i["i"] % 4]
            sc_i["i"] += 1
            return b

        for hg in range(4):
            def f(pool, hg=hg):
                if hg >= 1:
                    wt(pool, t_vprod[hg - 1])
                s_wv.inc(pool.dma_start(out=wvT[:, :], in_=wv_in[hg]), 16)
            blk.gpsimd(f)
            t_wv[hg] = s_wv.tok

            for h in range(4 * hg, 4 * hg + 4):
                def f(pool, h=h):
                    if h >= 1:
                        wt(pool, t_qkprod[h - 1])
                    s_wqk.inc(pool.dma_start(out=wqkT[:, 0:D], in_=wq_in[h]), 16)
                    s_wqk.inc(pool.dma_start(out=wqkT[:, D:2 * D], in_=wk_in[h]), 16)
                blk.gpsimd(f)
                t_wqk[h] = s_wqk.tok

                qk_banks = {}
                for step in range(9):
                    bank = next_sc()

                    def f(t, h=h, step=step, bank=bank):
                        if step == 0:
                            wt(t, t_wqk[h])
                            wt(t, t_xnfm)
                        wt(t, bank.freed)
                        if step < 4:
                            q4 = step
                            for dc in range(ND):
                                ins = t.matmul(bank.p[:, :], wqkT[:, dc * 128:(dc + 1) * 128],
                                               xnfm[:, dc * T + q4 * 512:dc * T + (q4 + 1) * 512],
                                               start=(dc == 0), stop=(dc == ND - 1))
                        elif step == 4:
                            for dc in range(ND):
                                ins = t.matmul(bank.p[:, 0:16],
                                               wqkT[:, D + dc * 128:D + (dc + 1) * 128],
                                               g_fmc[:, dc * G:(dc + 1) * G],
                                               start=(dc == 0), stop=(dc == ND - 1))
                        else:
                            q4 = step - 5
                            for dc in range(ND):
                                ins = t.matmul(bank.p[:, :], wqkT[:, D + dc * 128:D + (dc + 1) * 128],
                                               xnfm[:, dc * T + q4 * 512:dc * T + (q4 + 1) * 512],
                                               start=(dc == 0), stop=(dc == ND - 1))
                        bank.used = s_pe.inc(ins)
                    blk.tensor(f)
                    qk_banks[step] = bank

                    def f(dve, h=h, step=step, bank=bank):
                        wt(dve, bank.used)
                        if h >= 1:
                            wt(dve, t_lastscore.get(h - 1))
                        dve.wait_ge(s_dve.h, s_dve.n)
                        if step < 4:
                            ins = dve.tensor_scalar(qb[:, step * 512:(step + 1) * 512],
                                                    bank.p[:, :], QSCALE, None, ALU.mult)
                        elif step == 4:
                            ins = dve.tensor_copy(kb[:, 0:16], bank.p[:, 0:16])
                        else:
                            q4 = step - 5
                            ins = dve.tensor_copy(kb[:, 16 + q4 * 512:16 + (q4 + 1) * 512], bank.p[:, :])
                        bank.freed = s_dve.inc(ins)
                    blk.vector(f)
                t_qkprod[h] = s_pe.tok
                t_qk_evac[h] = s_dve.tok

                if h % 4 == 0:
                    for kt in range(NT + 1):
                        bank = next_sc()

                        def f(t, hg=hg, kt=kt, bank=bank):
                            if kt == 0:
                                wt(t, t_wv[hg])
                                wt(t, t_xnfm)
                            wt(t, bank.freed)
                            if kt < NT:
                                for dc in range(ND):
                                    ins = t.matmul(bank.p[:, :],
                                                   xnfm[:, dc * T + kt * 128:dc * T + (kt + 1) * 128],
                                                   wvT[:, dc * 512:(dc + 1) * 512],
                                                   start=(dc == 0), stop=(dc == ND - 1))
                            else:
                                for dc in range(ND):
                                    ins = t.matmul(bank.p[0:16, :],
                                                   g_fmc[:, dc * G:(dc + 1) * G],
                                                   wvT[:, dc * 512:(dc + 1) * 512],
                                                   start=(dc == 0), stop=(dc == ND - 1))
                            bank.used = s_pe.inc(ins)
                        blk.tensor(f)

                        def f(dve, hg=hg, kt=kt, bank=bank):
                            wt(dve, bank.used)
                            if hg >= 1:
                                wt(dve, t_pv.get((4 * hg - 1, 3)))
                            dve.wait_ge(s_dve.h, s_dve.n)
                            if kt < NT:
                                ins = dve.tensor_copy(v4[:, kt * 512:(kt + 1) * 512], bank.p[:, :])
                            else:
                                ins = dve.tensor_copy(v4[0:16, kt * 512:(kt + 1) * 512], bank.p[0:16, :])
                            bank.freed = s_dve.inc(ins)
                        blk.vector(f)
                    t_vprod[hg] = s_pe.tok
                    t_v_evac[hg] = s_dve.tok

                hh = h % 4
                t_exp = {}
                t_pvmm = {}
                for q4 in range(4):
                    klist = kts(q4) + ["glob"]
                    nk = len(klist)
                    for i, kt in enumerate(klist):
                        bank = next_sc()

                        def f(t, h=h, q4=q4, i=i, kt=kt, bank=bank):
                            if i == 0 and q4 == 0:
                                wt(t, t_qk_evac[h])
                                wt(t, t_v_evac[h // 4])
                            wt(t, bank.freed)
                            if kt == "glob":
                                ins = t.matmul(bank.p[0:16, :], kb[:, 0:16],
                                               qb[:, q4 * 512:(q4 + 1) * 512],
                                               start=True, stop=True)
                            else:
                                t.matmul(bank.p[:, :], kb[:, 16 + kt * 128:16 + (kt + 1) * 128],
                                         qb[:, q4 * 512:(q4 + 1) * 512], start=True, stop=False)
                                v = kt - 4 * q4 + 2
                                ins = t.matmul(bank.p[:, :], ident[:, :],
                                               masks[:, v * 512:(v + 1) * 512],
                                               start=False, stop=True)
                            bank.used = s_pe.inc(ins)
                            t_lastscore[h] = bank.used
                        blk.tensor(f)

                        _t_used = bank.used
                        _t_pb = (t_pvmm.get((h, q4, i - 4)) if i >= 4 else None) or \
                            t_pv.get((h, q4 - 1)) or t_pv.get((h - 1, 3))

                        def f(act, i=i, kt=kt, bank=bank, _t_used=_t_used, _t_pb=_t_pb):
                            wt(act, _t_used)
                            wt(act, _t_pb)
                            if kt == "glob":
                                return act.activation(p_g[:, :], bank.p[0:16, :], AF.Exp)
                            return act.activation(p_sb[:, (i % 4) * 512:(i % 4 + 1) * 512],
                                                  bank.p[:, :], AF.Exp)
                        bank.freed = da_emit(f)
                        t_exp[(h, q4, i)] = bank.freed

                        if i >= 1:
                            emit_pv(h, q4, i - 1, klist, hh, t_exp, t_pvmm, t_pv)
                    emit_pv(h, q4, nk - 1, klist, hh, t_exp, t_pvmm, t_pv)
                    t_pv[(h, q4)] = s_pe.tok
                    while pending_tail:
                        pending_tail.pop(0)()

                    def f(dve, h=h, q4=q4):
                        sl = slice(q4 * 512, (q4 + 1) * 512)
                        wt(dve, t_pv[(h, q4)])
                        dve.wait_ge(s_dve.h, s_dve.n)
                        s_dve.inc(dve.tensor_copy(rowA[:, sl], b_sr.p[0:1, :]))
                    blk.vector(f)
                    b_sr.freed = s_dve.tok

                    def f(dve, h=h, q4=q4):
                        sl = slice(q4 * 512, (q4 + 1) * 512)
                        dve.wait_ge(s_dve.h, s_dve.n)
                        s_dve.inc(dve.reciprocal(rowB[:, sl], rowA[:, sl]))
                        dve.wait_ge(s_dve.h, s_dve.n)
                        s_dve.inc(dve.tensor_copy(rowH[:, sl], rowB[:, sl]))
                        dve.wait_ge(s_dve.h, s_dve.n)
                        s_dve.inc(dve.tensor_tensor(rowC[:, sl], rowB[:, sl], rowH[:, sl],
                                                    ALU.subtract))
                        dve.wait_ge(s_dve.h, s_dve.n)
                        s_dve.inc(dve.tensor_copy(rowL[:, sl], rowC[:, sl]))
                    blk.vector(f)
                    t_rhl4 = s_dve.tok

                    def tail_fin(h=h, q4=q4, t_rhl4=t_rhl4):
                        def f(t):
                            wt(t, t_rhl4)
                            wt(t, b_rb.freed)
                            t.matmul(b_rb.p[:, :], ones_row[:, :], rowH[:, q4 * 512:(q4 + 1) * 512],
                                     start=True, stop=False)
                            ins = t.matmul(b_rb.p[:, :], ones_row[:, :], rowL[:, q4 * 512:(q4 + 1) * 512],
                                           start=False, stop=True)
                            b_rb.used = s_pe.inc(ins)
                        blk.tensor(f)

                        def f(dve):
                            wt(dve, b_rb.used)
                            wt(dve, t_ao.get((h, q4 - 1)) or t_ao.get((h - 1, 3)))
                            dve.wait_ge(s_dve.h, s_dve.n)
                            b_rb.freed = s_dve.inc(dve.tensor_copy(rb_sb[:, :], b_rb.p[:, :]))
                        blk.vector(f)
                        t_rb = s_dve.tok

                        def f(dve, t_rb=t_rb):
                            wt(dve, t_rb)
                            idx = (h * 4 + q4) % 2
                            wt(dve, t_ao_st.get((h * 4 + q4 - 2)))
                            ins = dve.tensor_tensor(aostg[:, idx * 512:(idx + 1) * 512],
                                                    b_pv[q4 % 2].p[:, :], rb_sb[:, :], ALU.mult)
                            s_dve.inc(ins)
                            b_pv[q4 % 2].freed = s_dve.tok
                        blk.vector(f)
                        t_ao[(h, q4)] = s_dve.tok

                        def f(sync, t_aoe=s_dve.tok):
                            wt(sync, t_aoe)
                            idx = (h * 4 + q4) % 2
                            dst = ao_dram[4 * q4:4 * q4 + 4, h].rearrange("j p k -> p j k")
                            srcv = aostg[:, idx * 512:(idx + 1) * 512].rearrange(
                                "p (j k) -> p j k", j=4)
                            s_ao[idx].inc(sync.dma_start(out=dst, in_=srcv), 16)
                        blk.sync(f)
                        t_ao_st[h * 4 + q4] = s_ao[(h * 4 + q4) % 2].tok

                    pending_tail.append(tail_fin)

        while pending_tail:
            pending_tail.pop(0)()
        barrier()
        pp.close()
        p2i.close()
        p1s.close()
        p2s.close()

        p3s = ExitStack()
        v1h = sba(p3s, "v1ht", [128, 8 * D], F32)
        woT = sba(p3s, "woTt", [128, 2 * ND * 1024], BF16)
        aot = sba(p3s, "aott", [128, 2 * D], BF16)
        xres = sba(p3s, "xrest", [128, 2 * D], F32)
        p3p = ExitStack()
        b_wo = [Bank(psa(p3p, "p_wo0", [128, 512])), Bank(psa(p3p, "p_wo1", [128, 512]))]

        t_spill_half = {}
        t_wold = {}
        t_womm = {}
        t_woev = {}
        t_aotld = {}
        t_ln2 = {}
        t_spill = {}
        t_xres = {}

        def f(pool):
            for sl in range(4):
                base = sl * ND * 512
                for ch in range(4):
                    c0 = ch * ND * 128
                    s_wos[sl].inc(pool.dma_start(
                        out=woT[:, base + c0:base + c0 + ND * 128],
                        in_=wo_in[sl][:, c0:c0 + ND * 128]), 16)
        blk.gpsimd(f)
        t_wosl = [s_wos[sl].tok for sl in range(4)]

        for th in range(2):
            for eh in range(2):

                for ti in range(8):
                    tt = th * 8 + ti
                    abuf = ti % 2

                    def f(sync, th=th, eh=eh, ti=ti, tt=tt, abuf=abuf):
                        prev = t_womm.get((th, eh, ti - 2)) or t_womm.get((th, eh - 1, 6 + ti)) \
                            or t_womm.get((th - 1, 1, 6 + ti))
                        wt(sync, prev)
                        src_ = ao_dram[tt].rearrange("h p k -> p h k")
                        dstv = aot[:, abuf * D:(abuf + 1) * D].rearrange(
                            "p (h k) -> p h k", h=NH)
                        s_aot[abuf].inc(sync.dma_start(out=dstv, in_=src_), 16)
                    blk.sync(f)
                    t_aotld[(th, eh, ti)] = s_aot[abuf].tok

                    for e2 in range(2):
                        e4 = 2 * eh + e2
                        bank = b_wo[e2]

                        def f(t, th=th, eh=eh, ti=ti, e2=e2, e4=e4, abuf=abuf, bank=bank):
                            if th == 0 and ti == 0:
                                wt(t, t_wosl[e4])
                            if e2 == 0:
                                wt(t, t_aotld[(th, eh, ti)])
                            wt(t, bank.freed)
                            off = eh * ND * 1024 + e2 * ND * 512
                            for hc in range(NH):
                                ins = t.matmul(bank.p[:, :],
                                               aot[:, abuf * D + hc * 128:abuf * D + (hc + 1) * 128],
                                               woT[:, off + hc * 512:off + (hc + 1) * 512],
                                               start=(hc == 0), stop=(hc == NH - 1))
                            bank.used = s_pe.inc(ins)
                            t_womm[(th, eh, ti)] = bank.used
                        blk.tensor(f)

                        def f(dve, th=th, eh=eh, ti=ti, e2=e2, e4=e4, bank=bank):
                            wt(dve, bank.used)
                            if eh == 0 and e2 == 0 and th >= 1:
                                wt(dve, t_spill_half.get(th - 1))
                            dve.wait_ge(s_dve.h, s_dve.n)
                            ins = dve.tensor_copy(v1h[:, ti * D + e4 * 512:ti * D + (e4 + 1) * 512],
                                                  bank.p[:, :])
                            bank.freed = s_dve.inc(ins)
                        blk.vector(f)
                        t_woev[(th, eh, ti, e2)] = s_dve.tok
                t_womm[(th, eh)] = s_pe.tok

            for ti in range(8):
                tt = th * 8 + ti
                buf = ti % 2

                def f(sync, th=th, ti=ti, tt=tt, buf=buf):
                    wt(sync, t_ln2.get((th, ti - 2)) or t_ln2.get((th - 1, 6 + ti)))
                    s_xr[buf].inc(sync.dma_start(out=xres[:, buf * D:(buf + 1) * D],
                                                 in_=x_in[tt * 128:(tt + 1) * 128, :]), 16)
                blk.sync(f)
                t_xres[(th, ti)] = s_xr[buf].tok

                def f(dve, th=th, ti=ti, buf=buf):
                    wt(dve, t_xres[(th, ti)])
                    wt(dve, t_woev[(th, 1, ti, 1)])
                    vv = v1h[:, ti * D:(ti + 1) * D]
                    dve.wait_ge(s_dve.h, s_dve.n)
                    s_dve.inc(dve.scalar_tensor_tensor(vv, xres[:, buf * D:(buf + 1) * D],
                                                       2.0, vv, ALU.mult, ALU.add))
                blk.vector(f)
                t_v1 = s_dve.tok

                t_rstd = emit_ln_stats(lambda ti=ti: v1h[:, ti * D:(ti + 1) * D], t_v1)

                def f(dve, th=th, ti=ti, t_rstd=t_rstd):
                    wt(dve, t_rstd)
                    vv = v1h[:, ti * D:(ti + 1) * D]
                    dve.wait_ge(s_dve.h, s_dve.n)
                    s_dve.inc(dve.tensor_scalar(vv, vv, agm[:, 0:1], rstd[:, 0:1],
                                                ALU.subtract, ALU.mult))
                blk.vector(f)
                t_ln2[(th, ti)] = s_dve.tok

                def f(sync, tt=tt, ti=ti, t_r2=s_dve.tok):
                    wt(sync, t_r2)
                    s_r2s.inc(sync.dma_start(out=r2_dram[tt * 128:(tt + 1) * 128, :],
                                              in_=v1h[:, ti * D:(ti + 1) * D]), 16)
                blk.sync(f)
                t_spill[(th, ti)] = s_r2s.tok
            t_spill_half[th] = s_r2s.tok

        barrier()
        p3p.close()
        p3s.close()

        p4s = ExitStack()
        hmidt = sba(p4s, "hmidt", [128, NHB * 1024], BF16)
        hmid = hmidt

        t_gu = {}
        t_hmid = {}
        t_hmid_all = {}
        t_dpe = {}
        t_ln3 = {}
        t_outst = {}
        t_v2ld = {}
        t_v2spst = {}
        t_cshl = {}
        t_hcshl = {}
        t_trall = {}
        t_trtile = {}
        t_r2rd = {}
        reb = {}

        def emit_rebuild_loads(p):
            pxs = ExitStack()
            xn2h = sba(pxs, "xn2ht%d" % p, [128, ND * 1024], BF16, is_transpose_dest=True)
            prb = ExitStack()
            n2p = sba(prb, "n2pt%d" % p, [128, 4 * D], BF16)
            reb[p] = dict(pxs=pxs, prb=prb, xn2h=xn2h, n2p=n2p, t_n2p={})

            def f(pool, p=p, n2p=n2p):
                if p > 0:
                    wt(pool, t_cshl[p - 1])
                for ti in range(4):
                    tt = p * 8 + ti
                    s_n2l[ti].inc(pool.dma_start(out=n2p[:, ti * D:(ti + 1) * D],
                                                 in_=r2_dram[tt * 128:(tt + 1) * 128, :]), 16)
            blk.gpsimd(f)
            for ti in range(4):
                reb[p]["t_n2p"][ti] = s_n2l[ti].tok

        def emit_late_load(p, ti):
            st = reb[p]

            def f(pool, p=p, ti=ti, n2p=st["n2p"]):
                wt(pool, t_trtile[(p, ti - 4)])
                tt = p * 8 + ti
                s_n2l[ti % 4].inc(pool.dma_start(out=n2p[:, (ti % 4) * D:(ti % 4 + 1) * D],
                                                 in_=r2_dram[tt * 128:(tt + 1) * 128, :]), 16)
            blk.gpsimd(f)
            st["t_n2p"][ti] = s_n2l[ti % 4].tok

        def emit_transpose_tile(p, ti):
            st = reb[p]

            def f(sync, p=p, ti=ti, xn2h=st["xn2h"], n2p=st["n2p"], t_ld=st["t_n2p"][ti]):
                wt(sync, t_ld)
                if ti == 0 and p > 0:
                    wt(sync, t_gu[(p - 1, NHB - 1)])
                    wt(sync, t_hmid[(p - 1, NHB - 1, 1)])
                for c in range(ND):
                    ins = sync.dma_start_transpose(
                        out=xn2h[:, c * 1024 + ti * 128:c * 1024 + (ti + 1) * 128],
                        in_=n2p[:, (ti % 4) * D + c * 128:(ti % 4) * D + (c + 1) * 128])
                    s_tr[ti].inc(ins, 16)
            blk.sync(f)
            t_trtile[(p, ti)] = s_tr[ti].tok

        def emit_colsum(p):
            xn2h = reb[p]["xn2h"]
            t_csev = {}
            for tc in range(2):
                def f(t, p=p, tc=tc, xn2h=xn2h):
                    if tc == 0:
                        for tok in t_trall[p]:
                            wt(t, tok)
                    wt(t, b_cs.freed)
                    for dc in range(ND):
                        ins = t.matmul(b_cs.p[0:1, :], ones_col[:, :],
                                       xn2h[:, dc * 1024 + tc * 512:dc * 1024 + (tc + 1) * 512],
                                       start=(dc == 0), stop=(dc == ND - 1))
                    b_cs.used = s_pe.inc(ins)
                blk.tensor(f)

                def f(dve, tc=tc):
                    wt(dve, b_cs.used)
                    dve.wait_ge(s_dve.h, s_dve.n)
                    b_cs.freed = s_dve.inc(dve.tensor_copy(rowA[:, tc * 512:(tc + 1) * 512],
                                                           b_cs.p[0:1, :]))
                blk.vector(f)
                t_csev[tc] = s_dve.tok

            def f(dve, p=p):
                wt(dve, t_csev[1])
                if p > 0:
                    wt(dve, t_dpe[(p - 1, 3)])
                dve.wait_ge(s_dve.h, s_dve.n)
                s_dve.inc(dve.tensor_copy(rowH[:, 0:1024], rowA[:, 0:1024]))
                dve.wait_ge(s_dve.h, s_dve.n)
                s_dve.inc(dve.tensor_tensor(rowC[:, 0:1024], rowA[:, 0:1024],
                                            rowH[:, 0:1024], ALU.subtract))
                dve.wait_ge(s_dve.h, s_dve.n)
                s_dve.inc(dve.tensor_copy(rowL[:, 0:1024], rowC[:, 0:1024]))
            blk.vector(f)
            t_cshl[p] = s_dve.tok

        def emit_gate_up(p, ln3_hook=None):
            xn2h = reb[p]["xn2h"]
            pgu = ExitStack()
            wgT = sba(pgu, "wgTt%d" % p, [128, 2 * D], BF16)
            wuT = sba(pgu, "wuTt%d" % p, [128, 2 * D], BF16)
            gus = sba(pgu, "gust%d" % p, [128, 1024], F32)
            t_wgld = {}
            t_gact = {}
            for hb in range(NHB):
                j = hb % 2

                def f(pool, p=p, hb=hb, j=j, wgT=wgT, wuT=wuT):
                    prev = t_gu.get((p, hb - 2)) or t_gu.get((p - 1, NHB - 2 + hb))
                    wt(pool, prev)
                    if hb < 2:
                        for tok in t_trall[p]:
                            wt(pool, tok)
                    s_wg[j].inc(pool.dma_start(out=wgT[:, j * D:(j + 1) * D], in_=wg_in[hb]), 16)
                    s_wg[j].inc(pool.dma_start(out=wuT[:, j * D:(j + 1) * D], in_=wu_in[hb]), 16)
                blk.gpsimd(f)
                t_wgld[hb] = s_wg[j].tok

                for tc in range(2):
                    for gu in range(2):
                        bank = b_gu[tc * 2 + gu]

                        def f(t, p=p, hb=hb, j=j, tc=tc, gu=gu, bank=bank,
                              wgT=wgT, wuT=wuT, xn2h=xn2h):
                            if tc == 0 and gu == 0:
                                wt(t, t_wgld[hb])
                                wt(t, t_cshl[p])
                            wt(t, bank.freed)
                            wsl = (wgT if gu == 0 else wuT)
                            for dc in range(ND):
                                t.matmul(bank.p[:, :], wsl[:, j * D + dc * 128:j * D + (dc + 1) * 128],
                                         xn2h[:, dc * 1024 + tc * 512:dc * 1024 + (tc + 1) * 512],
                                         start=(dc == 0), stop=False)
                            t.matmul(bank.p[:, :], neg128[:, 0:128],
                                     rowH[:, tc * 512:(tc + 1) * 512], start=False, stop=False)
                            ins = t.matmul(bank.p[:, :], neg128[:, 0:128],
                                           rowL[:, tc * 512:(tc + 1) * 512], start=False, stop=True)
                            bank.used = s_pe.inc(ins)
                        blk.tensor(f)
                    t_gu[(p, hb)] = s_pe.tok

                for tc in range(2):
                    bg = b_gu[tc * 2]
                    bu = b_gu[tc * 2 + 1]

                    _prev = t_hmid.get((p, hb, tc - 1)) if tc == 1 else \
                        (t_hmid.get((p, hb - 1, 1)) or t_hmid.get((p - 1, NHB - 1, 1)))

                    def f(act, hb=hb, bg=bg, gus=gus, _t_used=bg.used, _prev=_prev):
                        wt(act, _t_used)
                        wt(act, _prev)
                        return act.activation(gus[:, 0:512], bg.p[:, :], AF.Silu,
                                              scale=sgt[:, hb:hb + 1])
                    bg.freed = da_emit(f)
                    t_gact[(hb, tc)] = bg.freed

                    def f(dve, p=p, hb=hb, tc=tc, bu=bu, gus=gus):
                        wt(dve, bu.used)
                        if hb == 0 and tc == 0:
                            wt(dve, t_dpe.get((p - 1, 3)))
                        dve.wait_ge(s_dve.h, s_dve.n)
                        ins = dve.tensor_scalar(gus[:, 512:1024], bu.p[:, :],
                                                sut[:, hb:hb + 1], None, ALU.mult)
                        bu.freed = s_dve.inc(ins)
                        wt(dve, t_gact[(hb, tc)])
                        dve.wait_ge(s_dve.h, s_dve.n)
                        ins = dve.tensor_tensor(
                            hmid[:, hb * 1024 + tc * 512:hb * 1024 + (tc + 1) * 512],
                            gus[:, 0:512], gus[:, 512:1024], ALU.mult)
                        s_dve.inc(ins)
                    blk.vector(f)
                    t_hmid[(p, hb, tc)] = s_dve.tok

                if ln3_hook is not None and hb >= 2 and hb <= 30 and hb % 4 == 2:
                    ln3_hook((hb - 2) // 4)
            t_hmid_all[p] = s_dve.tok
            pgu.close()

        def emit_hcolsum(p):
            t_hcs_ev = {}
            for tc in range(2):
                def f(t, p=p, tc=tc):
                    wt(t, t_hmid_all[p])
                    wt(t, b_cs.freed)
                    for hb in range(NHB):
                        ins = t.matmul(b_cs.p[0:1, :], ones_col[:, :],
                                       hmid[:, hb * 1024 + tc * 512:hb * 1024 + (tc + 1) * 512],
                                       start=(hb == 0), stop=(hb == NHB - 1))
                    b_cs.used = s_pe.inc(ins)
                blk.tensor(f)

                def f(dve, tc=tc):
                    wt(dve, b_cs.used)
                    dve.wait_ge(s_dve.h, s_dve.n)
                    b_cs.freed = s_dve.inc(dve.tensor_copy(rowA[:, tc * 512:(tc + 1) * 512],
                                                           b_cs.p[0:1, :]))
                blk.vector(f)
                t_hcs_ev[tc] = s_dve.tok

            def f(dve):
                wt(dve, t_hcs_ev[1])
                dve.wait_ge(s_dve.h, s_dve.n)
                s_dve.inc(dve.tensor_copy(rowH[:, 0:1024], rowA[:, 0:1024]))
                dve.wait_ge(s_dve.h, s_dve.n)
                s_dve.inc(dve.tensor_tensor(rowC[:, 0:1024], rowA[:, 0:1024],
                                            rowH[:, 0:1024], ALU.subtract))
                dve.wait_ge(s_dve.h, s_dve.n)
                s_dve.inc(dve.tensor_copy(rowL[:, 0:1024], rowC[:, 0:1024]))
            blk.vector(f)
            t_hcshl[p] = s_dve.tok

        def emit_down(p, wdT, v2sp, tr_hook=None, ld_hook=None):
            t_wdld = {}
            for e4 in range(4):
                for hb in range(NHB):
                    j = hb % 2

                    def f(pool, p=p, e4=e4, hb=hb, j=j, wdT=wdT):
                        prev = t_dpe.get((p, e4, hb - 2)) or t_dpe.get((p, e4 - 1, NHB - 2 + hb)) \
                            or t_dpe.get((p - 1, 3, NHB - 2 + hb))
                        wt(pool, prev)
                        if e4 == 0 and hb < 2:
                            wt(pool, t_gu[(p, NHB - 1)])
                            wt(pool, t_hmid[(p, NHB - 1, 1)])
                            if p > 0:
                                wt(pool, t_outst[(p - 1, 7)])
                        s_wd[j].inc(pool.dma_start(out=wdT[:, j * 512:(j + 1) * 512],
                                                   in_=wd_in[e4, hb * 128:(hb + 1) * 128, :]), 16)
                    blk.gpsimd(f)
                    t_wdld[(e4, hb)] = s_wd[j].tok
                    if ld_hook is not None and e4 == 0 and hb >= 8 and hb % 4 == 0 and hb < 24:
                        ld_hook(4 + (hb - 8) // 4)

                    def f(t, p=p, e4=e4, hb=hb, j=j, wdT=wdT):
                        wt(t, t_wdld[(e4, hb)])
                        if hb == 0:
                            wt(t, t_hcshl[p])
                            for ti in range(8):
                                wt(t, pd_banks[ti].freed)
                        for ti in range(8):
                            ins = t.matmul(pd_banks[ti].p[:, :],
                                           hmid[:, hb * 1024 + ti * 128:hb * 1024 + (ti + 1) * 128],
                                           wdT[:, j * 512:(j + 1) * 512],
                                           start=(hb == 0), stop=False)
                        if hb == NHB - 1:
                            for ti in range(8):
                                t.matmul(pd_banks[ti].p[:, :],
                                         rowH[:, ti * 128:(ti + 1) * 128], neg128[:, :],
                                         start=False, stop=False)
                                ins = t.matmul(pd_banks[ti].p[:, :],
                                               rowL[:, ti * 128:(ti + 1) * 128], neg128[:, :],
                                               start=False, stop=True)
                                pd_banks[ti].used = s_pe.inc(ins)
                        else:
                            s_pe.inc(ins)
                        t_dpe[(p, e4, hb)] = s_pe.tok
                    blk.tensor(f)
                t_dpe[(p, e4)] = s_pe.tok

                for ti in range(8):
                    bank = pd_banks[ti]

                    def f(dve, p=p, e4=e4, ti=ti, bank=bank, v2sp=v2sp):
                        wt(dve, bank.used)
                        dve.wait_ge(s_dve.h, s_dve.n)
                        wt(dve, s_v2s[ti % 2].tok)
                        dst = v2sp[:, (ti % 2) * 512:(ti % 2 + 1) * 512]
                        ins = dve.tensor_tensor(dst, bank.p[:, :],
                                                sd_bc[:, e4 * 512:(e4 + 1) * 512], ALU.mult)
                        bank.freed = s_dve.inc(ins)
                    blk.vector(f)

                    def f(sync, p=p, e4=e4, ti=ti, t_e=s_dve.tok, v2sp=v2sp):
                        wt(sync, t_e)
                        s_v2s[ti % 2].inc(sync.dma_start(
                            out=v2_dram[p, e4, ti],
                            in_=v2sp[:, (ti % 2) * 512:(ti % 2 + 1) * 512]), 16)
                    blk.sync(f)
                    t_v2spst[(p, e4, ti)] = s_v2s[ti % 2].tok
                    if tr_hook is not None and e4 == 1 and ti % 2 == 1:
                        tr_hook(4 + ti // 2)

        def emit_ln3_tile(p, ti, v2a, r2tl, nsl):
            tt = p * 8 + ti
            buf = ti % nsl

            def f(sync, p=p, ti=ti, tt=tt, buf=buf, v2a=v2a, r2tl=r2tl):
                wt(sync, t_v2spst[(p, 3, ti)])
                wt(sync, t_outst.get((p, ti - 1)) or t_outst.get((p - 1, 7)))
                src = v2_dram[p, :, ti].rearrange("e p k -> p e k")
                dstv = v2a[:, :].rearrange("p (e k) -> p e k", e=4)
                s_v2l.inc(sync.dma_start(out=dstv, in_=src), 16)
                wt(sync, t_r2rd.get((p, ti - nsl)) or (t_ln3.get((p - 1, 7)) if p > 0 else None))
                s_r2l[buf].inc(sync.dma_start(out=r2tl[:, buf * D:(buf + 1) * D],
                                              in_=r2_dram[tt * 128:(tt + 1) * 128, :]), 16)
            blk.sync(f)
            t_v2ld[(p, ti)] = (s_v2l, s_v2l.n)
            t_r2ld3 = s_r2l[buf].tok

            def f(dve, p=p, ti=ti, buf=buf, t_r2ld3=t_r2ld3, v2a=v2a, r2tl=r2tl):
                wt(dve, t_v2ld[(p, ti)])
                wt(dve, t_r2ld3)
                dve.wait_ge(s_dve.h, s_dve.n)
                s_dve.inc(dve.tensor_tensor(v2a[:, :], v2a[:, :],
                                            r2tl[:, buf * D:(buf + 1) * D], ALU.add))
            blk.vector(f)
            t_v2 = s_dve.tok
            t_r2rd[(p, ti)] = t_v2

            t_rstd = emit_ln_stats(lambda v2a=v2a: v2a[:, :], t_v2)

            def f(dve, p=p, ti=ti, t_rstd=t_rstd, v2a=v2a):
                wt(dve, t_rstd)
                dve.wait_ge(s_dve.h, s_dve.n)
                s_dve.inc(dve.tensor_scalar(v2a[:, :], v2a[:, :], agm[:, 0:1], rstd[:, 0:1],
                                            ALU.subtract, ALU.mult))
            blk.vector(f)
            t_ln3[(p, ti)] = s_dve.tok

            def f(sync, tt=tt, t_o=s_dve.tok, v2a=v2a):
                wt(sync, t_o)
                s_out.inc(sync.dma_start(out=out_dram[tt * 128:(tt + 1) * 128, :],
                                          in_=v2a[:, :]), 16)
            blk.sync(f)
            t_outst[(p, ti)] = s_out.tok

        emit_rebuild_loads(0)
        ptr = ExitStack()
        trb = [Bank(es_psum_bf16(nc, ptr, "p_trz%d" % i)) for i in range(6)]
        for ti in range(8):
            if ti >= 4:
                emit_late_load(0, ti)
            st0 = reb[0]
            for c in range(ND):
                bank = trb[(ti * ND + c) % 6]

                def f(t, ti=ti, c=c, bank=bank, n2p=st0["n2p"], t_ld=st0["t_n2p"][ti]):
                    if c == 0:
                        wt(t, t_ld)
                    wt(t, bank.freed)
                    ins = t.transpose(bank.p[:, 0:128],
                                      n2p[:, (ti % 4) * D + c * 128:(ti % 4) * D + (c + 1) * 128],
                                      ident[:, :])
                    bank.used = s_pe.inc(ins)
                blk.tensor(f)

                def f(dve, ti=ti, c=c, bank=bank, xn2h=st0["xn2h"]):
                    wt(dve, bank.used)
                    dve.wait_ge(s_dve.h, s_dve.n)
                    ins = dve.tensor_copy(xn2h[:, c * 1024 + ti * 128:c * 1024 + (ti + 1) * 128],
                                          bank.p[:, 0:128])
                    bank.freed = s_dve.inc(ins)
                blk.vector(f)
            t_trtile[(0, ti)] = s_pe.tok
        ptr.close()
        t_trall[0] = [s_pe.tok, s_dve.tok]
        p4p = ExitStack()
        b_gu = [Bank(psa(p4p, "p_gu%d" % i, [128, 512])) for i in range(4)]
        b_cs = Bank(psa(p4p, "p_cs2", [128, 512]))
        b_s2 = [Bank(psa(p4p, "p_s20", [128, 512])), Bank(psa(p4p, "p_s21", [128, 512]))]
        b_x = Bank(psa(p4p, "p_x2", [128, 512]))
        pd_banks = b_gu + b_s2 + [b_cs, b_x]
        emit_colsum(0)
        reb[0]["prb"].close()
        emit_gate_up(0)
        reb[0]["pxs"].close()
        emit_hcolsum(0)

        pdn0 = ExitStack()
        wdT0 = sba(pdn0, "wdTt0", [128, 2 * 512], BF16)
        v2sp0 = sba(pdn0, "v2spt0", [128, 2 * 512], F32)
        v2a0 = sba(pdn0, "v2at0", [128, D], F32)
        r2t0 = sba(pdn0, "r2t0t", [128, D], F32)
        emit_rebuild_loads(1)
        for ti in range(4):
            emit_transpose_tile(1, ti)
        emit_down(0, wdT0, v2sp0, tr_hook=lambda k: emit_transpose_tile(1, k),
                  ld_hook=lambda k: emit_late_load(1, k))
        t_trall[1] = [s_tr[i].tok for i in range(8)]

        emit_colsum(1)
        reb[1]["prb"].close()
        emit_gate_up(1, ln3_hook=lambda ti: emit_ln3_tile(0, ti, v2a0, r2t0, 1))
        reb[1]["pxs"].close()
        pdn0.close()
        emit_hcolsum(1)

        pdn1 = ExitStack()
        wdT1 = sba(pdn1, "wdTt1", [128, 2 * 512], BF16)
        v2sp1 = sba(pdn1, "v2spt1", [128, 2 * 512], F32)
        v2a1 = sba(pdn1, "v2at1", [128, D], F32)
        r2t1 = sba(pdn1, "r2t1t", [128, 2 * D], F32)
        emit_down(1, wdT1, v2sp1)
        for ti in range(8):
            emit_ln3_tile(1, ti, v2a1, r2t1, 2)
        pdn1.close()
        p4p.close()
        p4s.close()

        flush_act()

        def f(sync):
            sync.wait_ge(s_out.h, s_out.n)
        blk.sync(f)

    outer.close()
    return nc


def emit_pv(h, q4, i, klist, hh, t_exp, t_pvmm, t_pv):
    blk = emit_pv.blk
    s_pe = emit_pv.s_pe
    b_pv = emit_pv.b_pv
    b_sr = emit_pv.b_sr
    v4 = emit_pv.v4
    p_sb = emit_pv.p_sb
    p_g = emit_pv.p_g
    ones_col = emit_pv.ones_col
    kt = klist[i]
    nk = len(klist)
    bpv = b_pv[q4 % 2]

    def f(t):
        wt(t, t_exp[(h, q4, i)])
        if i == 0:
            wt(t, bpv.freed)
            wt(t, b_sr.freed)
        first = (i == 0)
        last = (i == nk - 1)
        if kt == "glob":
            t.matmul(bpv.p[:, :], v4[0:16, 16 * 512 + hh * 128:16 * 512 + (hh + 1) * 128],
                     p_g[:, :], start=first, stop=last)
            ins = t.matmul(b_sr.p[0:1, :], ones_col[0:16, :], p_g[:, :],
                           start=first, stop=last)
        else:
            t.matmul(bpv.p[:, :], v4[:, kt * 512 + hh * 128:kt * 512 + (hh + 1) * 128],
                     p_sb[:, (i % 4) * 512:(i % 4 + 1) * 512], start=first, stop=last)
            ins = t.matmul(b_sr.p[0:1, :], ones_col[:, :],
                           p_sb[:, (i % 4) * 512:(i % 4 + 1) * 512], start=first, stop=last)
        tok = s_pe.inc(ins)
        t_pvmm[(h, q4, i)] = tok
        if last:
            bpv.used = tok
            b_sr.used = tok
    blk.tensor(f)


def host_prepare(L0_attn_x, L0_attn_global_tokens, L0_attn_ln_scale, L0_attn_ln_bias,
                 L0_attn_Wq, L0_attn_Wk, L0_attn_Wv, L0_attn_Wo,
                 L0_ffn_wq_gate, L0_ffn_s_gate, L0_ffn_wq_up, L0_ffn_s_up,
                 L0_ffn_wq_down, L0_ffn_s_down):
    ac = np.ascontiguousarray
    x = np.asarray(L0_attn_x, dtype=np.float32)
    g = ac(np.asarray(L0_attn_global_tokens, dtype=np.float32).reshape(G, D))
    lnw = ac(np.asarray(L0_attn_ln_scale, np.float32).reshape(ND, 128).T)
    lnb = ac(np.asarray(L0_attn_ln_bias, np.float32).reshape(ND, 128).T)
    def warr(w, nslice, csz):
        return ac(np.asarray(w, np.float32).T.reshape(ND, 128, nslice, csz)
                  .transpose(2, 1, 0, 3).reshape(nslice, 128, ND * csz))

    def warr_i(w, nslice, csz):
        return ac(np.asarray(w, np.int32).T.reshape(ND, 128, nslice, csz)
                  .transpose(2, 1, 0, 3).reshape(nslice, 128, ND * csz).astype(np.int32))

    wq = warr(L0_attn_Wq, NH, HD)
    wk = warr(L0_attn_Wk, NH, HD)
    wv = warr(L0_attn_Wv, 4, 512)
    wo = warr(L0_attn_Wo, 4, 512)
    wg = warr_i(L0_ffn_wq_gate, NHB, 128)
    wu = warr_i(L0_ffn_wq_up, NHB, 128)
    wd = ac(np.asarray(L0_ffn_wq_down, np.int32).T.reshape(HID, 4, 512).transpose(1, 0, 2))
    sg = ac(np.asarray(L0_ffn_s_gate, np.float32).reshape(NHB, 128).T)
    su = ac(np.asarray(L0_ffn_s_up, np.float32).reshape(NHB, 128).T)
    sd = ac(np.asarray(L0_ffn_s_down, np.float32).reshape(1, D))

    ident = np.eye(128, dtype=bfdt)
    ones_col = np.ones((128, 1), dtype=bfdt)
    ones_row = np.ones((1, 128), dtype=bfdt)
    neg128 = np.full((1, 512), -128.0, dtype=bfdt)
    pidx = np.arange(128)[:, None]
    fidx = np.arange(512)[None, :]
    m6 = np.empty((128, 6 * 512), dtype=bfdt)
    for iv in range(6):
        v = iv - 2
        diff = fidx - v * 128 - pidx
        valid = (diff >= 0) & (diff <= WIN - 1)
        m6[:, iv * 512:(iv + 1) * 512] = np.where(valid, 0.0, NEG).astype(bfdt)

    shared = dict(g=g, lnw=lnw, lnb=lnb, wq=wq, wk=wk, wv=wv, wo=wo,
                  wg=wg, wu=wu, wd=wd, sgv=sg, suv=su, sdv=sd,
                  ident=ident, mask6=m6, ones_col=ones_col, ones_row=ones_row,
                  neg128=neg128)
    in_maps = []
    for core in range(8):
        m = dict(shared)
        m["x"] = ac(x[core])
        in_maps.append(m)
    return in_maps


_NC_CACHE = {}


def kernel(**inputs):
    in_maps = host_prepare(**inputs)
    if "nc" not in _NC_CACHE:
        _NC_CACHE["nc"] = build_nc()
    nc = _NC_CACHE["nc"]
    res = run_bass_kernel_spmd(nc, in_maps, core_ids=list(range(8)))
    out = np.stack([np.asarray(res.results[i]["out"]) for i in range(8)], axis=0)
    return out.astype(np.float32)


# version 18
# speedup vs baseline: 1.1696x; 1.1696x over previous
import numpy as np
import ml_dtypes
import concourse.bass as bass
import concourse.mybir as mybir
from concourse.bass_utils import run_bass_kernel_spmd

F32 = mybir.dt.float32
BF16 = mybir.dt.bfloat16
I32 = mybir.dt.int32
AF = mybir.ActivationFunctionType
ALU = mybir.AluOpType
bfdt = ml_dtypes.bfloat16

D = 2048
T = 2048
NH = 16
HD = 128
G = 16
HID = 5632
NHB = HID // 128
WIN = 256
NT = T // 128
ND = D // 128
EPS = 1e-5
QSCALE = float(HD) ** -0.5
NEG = -30000.0


class Sem:
    def __init__(self, nc, name):
        self.g = nc.semaphore(name)
        self.h = self.g.__enter__()
        self.n = 0

    def inc(self, instr, k=1):
        instr.then_inc(self.h, k)
        self.n += k
        return (self, self.n)

    @property
    def tok(self):
        return (self, self.n)


def _wait_only(act, f):
    f(act)
    return None


def wt(eng, token):
    if token is None:
        return
    sem, n = token
    if n > 0:
        eng.wait_ge(sem.h, n)


class Bank:

    def __init__(self, psum):
        self.p = psum
        self.freed = None
        self.used = None


def build_nc(debug=False):
    nc = bass.Bass("TRN2", target_bir_lowering=False)

    x_in = nc.dram_tensor("x", [T, D], F32, kind="ExternalInput").ap()
    g_in = nc.dram_tensor("g", [G, D], F32, kind="ExternalInput").ap()
    lnw_in = nc.dram_tensor("lnw", [128, ND], F32, kind="ExternalInput").ap()
    lnb_in = nc.dram_tensor("lnb", [128, ND], F32, kind="ExternalInput").ap()
    wq_in = nc.dram_tensor("wq", [NH, 128, D], F32, kind="ExternalInput").ap()
    wk_in = nc.dram_tensor("wk", [NH, 128, D], F32, kind="ExternalInput").ap()
    wv_in = nc.dram_tensor("wv", [4, 128, ND * 512], F32, kind="ExternalInput").ap()
    wo_in = nc.dram_tensor("wo", [4, 128, ND * 512], F32, kind="ExternalInput").ap()
    wg_in = nc.dram_tensor("wg", [NHB, 128, D], I32, kind="ExternalInput").ap()
    wu_in = nc.dram_tensor("wu", [NHB, 128, D], I32, kind="ExternalInput").ap()
    wd_in = nc.dram_tensor("wd", [4, HID, 512], I32, kind="ExternalInput").ap()
    sg_in = nc.dram_tensor("sgv", [128, NHB], F32, kind="ExternalInput").ap()
    su_in = nc.dram_tensor("suv", [128, NHB], F32, kind="ExternalInput").ap()
    sd_in = nc.dram_tensor("sdv", [1, D], F32, kind="ExternalInput").ap()
    id_in = nc.dram_tensor("ident", [128, 128], BF16, kind="ExternalInput").ap()
    mk_in = nc.dram_tensor("mask6", [128, 6 * 512], BF16, kind="ExternalInput").ap()
    oc_in = nc.dram_tensor("ones_col", [128, 1], BF16, kind="ExternalInput").ap()
    or_in = nc.dram_tensor("ones_row", [1, 128], BF16, kind="ExternalInput").ap()
    n128_in = nc.dram_tensor("neg128", [1, 512], BF16, kind="ExternalInput").ap()

    out_dram = nc.dram_tensor("out", [T, D], F32, kind="ExternalOutput").ap()
    scratch_kind = "ExternalOutput" if debug else "Internal"
    r2_dram = nc.dram_tensor("r2sp", [T, D], F32, kind=scratch_kind).ap()
    ao_dram = nc.dram_tensor("aosp", [NT, NH, 128, 128], BF16, kind=scratch_kind).ap()
    v2_dram = nc.dram_tensor("v2sp", [2, 4, 8, 128, 512], F32, kind="Internal").ap()
    xn_tap = None
    if debug:
        xn_tap = nc.dram_tensor("xnfmd", [128, ND * T], BF16, kind="ExternalOutput").ap()

    s_pe = Sem(nc, "s_pe")
    s_dve = Sem(nc, "s_dve")
    s_act = Sem(nc, "s_act")
    s_c0 = Sem(nc, "s_c0")
    s_cg = Sem(nc, "s_cg")
    s_cgx = Sem(nc, "s_cgx")
    s_x = [Sem(nc, "s_x0"), Sem(nc, "s_x1")]
    s_xb1 = [Sem(nc, "s_xb1a"), Sem(nc, "s_xb1b")]
    s_wqk = Sem(nc, "s_wqk")
    s_wv = Sem(nc, "s_wv")
    s_ao = [Sem(nc, "s_ao0"), Sem(nc, "s_ao1")]
    s_wo = [Sem(nc, "s_wo0"), Sem(nc, "s_wo1")]
    s_aot = [Sem(nc, "s_aot0"), Sem(nc, "s_aot1")]
    s_xr = [Sem(nc, "s_xr0"), Sem(nc, "s_xr1")]
    s_r2s = Sem(nc, "s_r2s")
    s_r2l = [Sem(nc, "s_r2l0"), Sem(nc, "s_r2l1")]
    s_n2l = [Sem(nc, "s_n2l%d" % i) for i in range(4)]
    s_tr = [Sem(nc, "s_tr%d" % i) for i in range(8)]
    s_wos = [Sem(nc, "s_wos%d" % i) for i in range(4)]
    s_xb2 = [Sem(nc, "s_xb2a"), Sem(nc, "s_xb2b")]
    s_wg = [Sem(nc, "s_wg0"), Sem(nc, "s_wg1")]
    s_wd = [Sem(nc, "s_wd0"), Sem(nc, "s_wd1")]
    s_v2s = [Sem(nc, "s_v2s0"), Sem(nc, "s_v2s1")]
    s_v2l = Sem(nc, "s_v2l")
    s_out = Sem(nc, "s_out")
    all_sems = [s_pe, s_dve, s_act, s_c0, s_cg, s_cgx, s_wqk, s_wv, s_r2s,
                s_v2l, s_out] + s_x + s_xb1 + s_ao + s_wo + s_aot + s_xr +         s_r2l + s_n2l + s_tr + s_wos + s_xb2 + s_wg + s_wd + s_v2s

    from contextlib import ExitStack
    outer = ExitStack()

    def sba(stk, name, shape, dt, **kw):
        return stk.enter_context(nc.sbuf_tensor(name, shape, dt, **kw))

    def psa(stk, name, shape):
        return stk.enter_context(nc.psum_tensor(name, shape, F32))

    def es_psum_bf16(nc_, stk, name):
        return stk.enter_context(nc_.psum_tensor(name, [128, 128], BF16))

    ones_col = sba(outer, "ones_colt", [128, 1], BF16)
    ones_row = sba(outer, "ones_rowt", [1, 128], BF16)
    neg128 = sba(outer, "neg128t", [1, 512], BF16)
    lnw = sba(outer, "lnwt", [128, ND], F32)
    lnb = sba(outer, "lnbt", [128, ND], F32)
    sgt = sba(outer, "sgtt", [128, NHB], F32)
    sut = sba(outer, "sutt", [128, NHB], F32)
    stats = sba(outer, "statst", [128, 24], F32)
    agm = sba(outer, "agt", [128, 2], F32)
    rstd = sba(outer, "rstdt", [128, 1], F32)
    sqv = sba(outer, "sqvt", [128, 1], F32)
    rowA = sba(outer, "rowAt", [1, T], F32)
    rowB = sba(outer, "rowBt", [1, T], F32)
    rowC = sba(outer, "rowCt", [1, T], F32)
    rowH = sba(outer, "rowHt", [1, T], BF16)
    rowL = sba(outer, "rowLt", [1, T], BF16)
    rb_sb = sba(outer, "rb_sbt", [128, 512], F32)
    ident = sba(outer, "identt", [128, 128], BF16)
    epsv = sba(outer, "epsvt", [128, 1], F32)
    sd_bc = sba(outer, "sd_bct", [128, D], F32)

    act_q = []

    def da_emit(fn, inc=1):
        s_act.n += inc
        act_q.append((fn, s_act.n, inc))
        return (s_act, s_act.n)

    with nc.Block() as blk:

        def flush_act():
            q = list(act_q)
            act_q.clear()
            if not q:
                return

            def body(act):
                for fn, n_after, inc in q:
                    ins = fn(act)
                    if inc:
                        ins.then_inc(s_act.h, inc)
            blk.scalar(body)

        def barrier():
            toks = [(s.h, s.n) for s in all_sems]

            def f(eng):
                for h, n in toks:
                    if n > 0:
                        eng.wait_ge(h, n)
            blk.tensor(f)
            blk.vector(f)
            da_emit(lambda act, f=f: f(act) or act.engine_nop() if False else _wait_only(act, f), 0)
            blk.sync(f)
            blk.gpsimd(f)

        p2s = ExitStack()
        masks = sba(p2s, "maskt", [128, 6 * 512], BF16)
        g_bf = sba(p2s, "g_bft", [G, D], BF16)
        g_fmc = sba(p2s, "g_fmct", [128, G * ND], BF16)
        gtr_ev = {}
        sd_row = sba(p2s, "sd_rowt", [1, D], F32)

        def f(sync):
            for dst, src in [(ident, id_in), (masks, mk_in), (ones_col, oc_in),
                             (ones_row, or_in), (neg128, n128_in), (lnw, lnw_in),
                             (lnb, lnb_in), (sgt, sg_in), (sut, su_in),
                             (sd_row, sd_in)]:
                s_c0.inc(sync.dma_start(out=dst[:, :], in_=src[:, :]), 16)
        blk.sync(f)
        t_consts = s_c0.tok

        def f(pool):
            s_cg.inc(pool.dma_start(out=g_bf[:, :], in_=g_in[:, :]), 16)
        blk.gpsimd(f)
        t_gbf = s_cg.tok

        def f(dve):
            s_dve.inc(dve.memset(epsv[:, :], EPS))
        blk.vector(f)

        with nc.psum_tensor("p_gtr", [128, 128], BF16) as p_gtr:
            for c in range(ND):
                def f(t, c=c):
                    if c == 0:
                        wt(t, t_gbf)
                        wt(t, t_consts)
                    else:
                        wt(t, (s_dve, gtr_ev[c - 1]))
                    ins = t.transpose(p_gtr[0:128, 0:16], g_bf[:, c * 128:(c + 1) * 128],
                                      ident[0:16, 0:16])
                    s_pe.inc(ins)
                blk.tensor(f)
                t_m = s_pe.tok

                def f(dve, c=c, t_m=t_m):
                    wt(dve, t_m)
                    dve.wait_ge(s_dve.h, s_dve.n)
                    s_dve.inc(dve.tensor_copy(g_fmc[:, c * G:(c + 1) * G], p_gtr[0:128, 0:16]))
                blk.vector(f)
                gtr_ev[c] = s_dve.n
        t_gfmc = s_dve.tok

        def f(dve):
            wt(dve, t_consts)
            s_dve.inc(dve.tensor_copy(rowH[:, 0:D], sd_row[:, :]))
            dve.wait_ge(s_dve.h, s_dve.n)
            s_dve.inc(dve.tensor_tensor(rowC[:, 0:D], sd_row[:, :], rowH[:, 0:D], ALU.subtract))
            dve.wait_ge(s_dve.h, s_dve.n)
            s_dve.inc(dve.tensor_copy(rowL[:, 0:D], rowC[:, 0:D]))
        blk.vector(f)
        t_sdhl = s_dve.tok

        with nc.psum_tensor("p_sdb", [128, 512], F32) as p_sdb:
            sdb_toks = []
            for j in range(4):
                def f(t, j=j):
                    wt(t, t_sdhl)
                    if sdb_toks:
                        wt(t, sdb_toks[-1])
                    t.matmul(p_sdb[:, :], ones_row[:, :], rowH[:, j * 512:(j + 1) * 512],
                             start=True, stop=False)
                    ins = t.matmul(p_sdb[:, :], ones_row[:, :], rowL[:, j * 512:(j + 1) * 512],
                                   start=False, stop=True)
                    s_pe.inc(ins)
                blk.tensor(f)
                t_mm = s_pe.tok

                def f(dve, j=j, t_mm=t_mm):
                    wt(dve, t_mm)
                    dve.wait_ge(s_dve.h, s_dve.n)
                    s_dve.inc(dve.tensor_copy(sd_bc[:, j * 512:(j + 1) * 512], p_sdb[:, :]))
                blk.vector(f)
                sdb_toks.append(s_dve.tok)

            pass

        p1s = ExitStack()
        xnfm = sba(p1s, "xnfmt", [128, ND * T], BF16)
        p1i = ExitStack()
        xtile = sba(p1i, "xtilet", [128, 2 * D], F32)
        n1 = sba(p1i, "n1t", [128, 2 * D], BF16)
        p1p = ExitStack()
        trb = [Bank(es_psum_bf16(nc, p1p, "p_tr0")), Bank(es_psum_bf16(nc, p1p, "p_tr1"))]

        n1_tok = {}
        xb_tok = {}
        ln_act_tok = {"prev": None}
        ln_dve_tok = {"prev": None}

        def emit_ln_stats(get_v, tok_ready):
            def f(dve):
                wt(dve, tok_ready)
                wt(dve, ln_act_tok["prev"])
                dve.wait_ge(s_dve.h, s_dve.n)
                vv = get_v()
                for i in range(4):
                    ins = dve.bn_stats(stats[:, i * 6:(i + 1) * 6], vv[:, i * 512:(i + 1) * 512])
                s_dve.inc(ins)
                dve.wait_ge(s_dve.h, s_dve.n)
                s_dve.inc(dve.bn_aggr(agm[:, :], stats[:, :]))
            blk.vector(f)
            t_ag = s_dve.tok

            def f(act, t_ag=t_ag, t_prev=ln_dve_tok["prev"]):
                wt(act, t_ag)
                wt(act, t_prev)
                return act.activation(sqv[:, :], agm[:, 1:2], AF.Sqrt, bias=epsv[:, 0:1])
            t_sq = da_emit(f)
            ln_act_tok["prev"] = t_sq

            def f(dve):
                wt(dve, t_sq)
                dve.wait_ge(s_dve.h, s_dve.n)
                s_dve.inc(dve.reciprocal(rstd[:, :], sqv[:, :]))
            blk.vector(f)
            ln_dve_tok["prev"] = s_dve.tok
            return s_dve.tok

        xt_tok = {}
        for tt in range(NT):
            buf = tt % 2
            hstep = tt % 8

            def f(sync, tt=tt, buf=buf):
                if tt >= 2:
                    wt(sync, n1_tok[tt - 2])
                s_x[buf].inc(sync.dma_start(out=xtile[:, buf * D:(buf + 1) * D],
                                            in_=x_in[tt * 128:(tt + 1) * 128, :]), 16)
            blk.sync(f)
            xt_tok[tt] = s_x[buf].tok

            t_rstd = emit_ln_stats(lambda buf=buf: xtile[:, buf * D:(buf + 1) * D], xt_tok[tt])

            def f(dve, tt=tt, buf=buf, t_rstd=t_rstd):
                wt(dve, t_rstd)
                if tt >= 2:
                    wt(dve, xb_tok[tt - 2])
                s_dve.inc(dve.tensor_scalar(n1[:, buf * D:(buf + 1) * D],
                                            xtile[:, buf * D:(buf + 1) * D],
                                            agm[:, 0:1], rstd[:, 0:1],
                                            ALU.subtract, ALU.mult))
            blk.vector(f)
            n1_tok[tt] = s_dve.tok

            for c in range(ND):
                bank = trb[c % 2]

                def f(t, tt=tt, buf=buf, c=c, bank=bank):
                    if c == 0:
                        wt(t, n1_tok[tt])
                        if tt == 0:
                            wt(t, t_consts)
                    wt(t, bank.freed)
                    ins = t.transpose(bank.p[:, 0:128],
                                      n1[:, buf * D + c * 128:buf * D + (c + 1) * 128],
                                      ident[:, :])
                    bank.used = s_pe.inc(ins)
                blk.tensor(f)

                def f(dve, tt=tt, c=c, bank=bank):
                    wt(dve, bank.used)
                    dve.wait_ge(s_dve.h, s_dve.n)
                    ins = dve.tensor_scalar(xnfm[:, c * T + tt * 128:c * T + (tt + 1) * 128],
                                            bank.p[:, 0:128], lnw[:, c:c + 1], lnb[:, c:c + 1],
                                            ALU.mult, ALU.add)
                    bank.freed = s_dve.inc(ins)
                blk.vector(f)
            xb_tok[tt] = s_pe.tok
        t_xnfm = s_dve.tok

        if debug:
            def f(sync):
                wt(sync, t_xnfm)
                s_out.inc(sync.dma_start(out=xn_tap[:, :], in_=xnfm[:, :]), 16)
            blk.sync(f)

        p1p.close()
        p1i.close()
        barrier()

        p2i = ExitStack()
        qb = sba(p2i, "qbt", [128, T], BF16)
        kb = sba(p2i, "kbt", [128, G + T], BF16)
        v4 = sba(p2i, "v4t", [128, 17 * 512], BF16)
        wqkT = sba(p2i, "wqkTt", [128, 2 * D], BF16)
        wvT = sba(p2i, "wvTt", [128, ND * 512], BF16)
        p_sb = sba(p2i, "p_sbt", [128, 4 * 512], BF16)
        p_g = sba(p2i, "p_gt", [16, 512], BF16)
        aostg = sba(p2i, "aostgt", [128, 2 * 512], BF16)

        pp = ExitStack()
        b_sc = [Bank(psa(pp, f"p_sc{i}", [128, 512])) for i in range(4)]
        b_pv = [Bank(psa(pp, f"p_pv{i}", [128, 512])) for i in range(2)]
        b_sr = Bank(psa(pp, "p_srow", [128, 512]))
        b_rb = Bank(psa(pp, "p_rbc", [128, 512]))

        emit_pv.blk = blk
        emit_pv.s_pe = s_pe
        emit_pv.b_pv = b_pv
        emit_pv.b_sr = b_sr
        emit_pv.v4 = v4
        emit_pv.p_sb = p_sb
        emit_pv.p_g = p_g
        emit_pv.ones_col = ones_col

        t_wqk = {}
        t_wv = {}
        t_qkprod = {}
        t_qk_evac = {}
        t_vprod = {}
        t_v_evac = {}
        t_lastscore = {}
        t_pv = {}
        t_ao = {}
        t_ao_st = {}
        t_rbfree = {}

        pending_tail = []

        def kts(q4):
            return list(range(max(0, 4 * q4 - 2), min(NT, 4 * q4 + 4)))

        sc_i = {"i": 0}

        def next_sc():
            b = b_sc[sc_i["i"] % 4]
            sc_i["i"] += 1
            return b

        for hg in range(4):
            def f(pool, hg=hg):
                if hg >= 1:
                    wt(pool, t_vprod[hg - 1])
                s_wv.inc(pool.dma_start(out=wvT[:, :], in_=wv_in[hg]), 16)
            blk.gpsimd(f)
            t_wv[hg] = s_wv.tok

            for h in range(4 * hg, 4 * hg + 4):
                def f(pool, h=h):
                    if h >= 1:
                        wt(pool, t_qkprod[h - 1])
                    s_wqk.inc(pool.dma_start(out=wqkT[:, 0:D], in_=wq_in[h]), 16)
                    s_wqk.inc(pool.dma_start(out=wqkT[:, D:2 * D], in_=wk_in[h]), 16)
                blk.gpsimd(f)
                t_wqk[h] = s_wqk.tok

                qk_banks = {}
                for step in range(9):
                    bank = next_sc()

                    def f(t, h=h, step=step, bank=bank):
                        if step == 0:
                            wt(t, t_wqk[h])
                            wt(t, t_xnfm)
                        wt(t, bank.freed)
                        if step < 4:
                            q4 = step
                            for dc in range(ND):
                                ins = t.matmul(bank.p[:, :], wqkT[:, dc * 128:(dc + 1) * 128],
                                               xnfm[:, dc * T + q4 * 512:dc * T + (q4 + 1) * 512],
                                               start=(dc == 0), stop=(dc == ND - 1))
                        elif step == 4:
                            for dc in range(ND):
                                ins = t.matmul(bank.p[:, 0:16],
                                               wqkT[:, D + dc * 128:D + (dc + 1) * 128],
                                               g_fmc[:, dc * G:(dc + 1) * G],
                                               start=(dc == 0), stop=(dc == ND - 1))
                        else:
                            q4 = step - 5
                            for dc in range(ND):
                                ins = t.matmul(bank.p[:, :], wqkT[:, D + dc * 128:D + (dc + 1) * 128],
                                               xnfm[:, dc * T + q4 * 512:dc * T + (q4 + 1) * 512],
                                               start=(dc == 0), stop=(dc == ND - 1))
                        bank.used = s_pe.inc(ins)
                    blk.tensor(f)
                    qk_banks[step] = bank

                    def f(dve, h=h, step=step, bank=bank):
                        wt(dve, bank.used)
                        if h >= 1:
                            wt(dve, t_lastscore.get(h - 1))
                        dve.wait_ge(s_dve.h, s_dve.n)
                        if step < 4:
                            ins = dve.tensor_scalar(qb[:, step * 512:(step + 1) * 512],
                                                    bank.p[:, :], QSCALE, None, ALU.mult)
                        elif step == 4:
                            ins = dve.tensor_copy(kb[:, 0:16], bank.p[:, 0:16])
                        else:
                            q4 = step - 5
                            ins = dve.tensor_copy(kb[:, 16 + q4 * 512:16 + (q4 + 1) * 512], bank.p[:, :])
                        bank.freed = s_dve.inc(ins)
                    blk.vector(f)
                t_qkprod[h] = s_pe.tok
                t_qk_evac[h] = s_dve.tok

                if h % 4 == 0:
                    for kt in range(NT + 1):
                        bank = next_sc()

                        def f(t, hg=hg, kt=kt, bank=bank):
                            if kt == 0:
                                wt(t, t_wv[hg])
                                wt(t, t_xnfm)
                            wt(t, bank.freed)
                            if kt < NT:
                                for dc in range(ND):
                                    ins = t.matmul(bank.p[:, :],
                                                   xnfm[:, dc * T + kt * 128:dc * T + (kt + 1) * 128],
                                                   wvT[:, dc * 512:(dc + 1) * 512],
                                                   start=(dc == 0), stop=(dc == ND - 1))
                            else:
                                for dc in range(ND):
                                    ins = t.matmul(bank.p[0:16, :],
                                                   g_fmc[:, dc * G:(dc + 1) * G],
                                                   wvT[:, dc * 512:(dc + 1) * 512],
                                                   start=(dc == 0), stop=(dc == ND - 1))
                            bank.used = s_pe.inc(ins)
                        blk.tensor(f)

                        def f(dve, hg=hg, kt=kt, bank=bank):
                            wt(dve, bank.used)
                            if hg >= 1:
                                wt(dve, t_pv.get((4 * hg - 1, 3)))
                            dve.wait_ge(s_dve.h, s_dve.n)
                            if kt < NT:
                                ins = dve.tensor_copy(v4[:, kt * 512:(kt + 1) * 512], bank.p[:, :])
                            else:
                                ins = dve.tensor_copy(v4[0:16, kt * 512:(kt + 1) * 512], bank.p[0:16, :])
                            bank.freed = s_dve.inc(ins)
                        blk.vector(f)
                    t_vprod[hg] = s_pe.tok
                    t_v_evac[hg] = s_dve.tok

                hh = h % 4
                t_exp = {}
                t_pvmm = {}
                for q4 in range(4):
                    klist = kts(q4) + ["glob"]
                    nk = len(klist)
                    for i, kt in enumerate(klist):
                        bank = next_sc()

                        def f(t, h=h, q4=q4, i=i, kt=kt, bank=bank):
                            if i == 0 and q4 == 0:
                                wt(t, t_qk_evac[h])
                                wt(t, t_v_evac[h // 4])
                            wt(t, bank.freed)
                            if kt == "glob":
                                ins = t.matmul(bank.p[0:16, :], kb[:, 0:16],
                                               qb[:, q4 * 512:(q4 + 1) * 512],
                                               start=True, stop=True)
                            else:
                                t.matmul(bank.p[:, :], kb[:, 16 + kt * 128:16 + (kt + 1) * 128],
                                         qb[:, q4 * 512:(q4 + 1) * 512], start=True, stop=False)
                                v = kt - 4 * q4 + 2
                                ins = t.matmul(bank.p[:, :], ident[:, :],
                                               masks[:, v * 512:(v + 1) * 512],
                                               start=False, stop=True)
                            bank.used = s_pe.inc(ins)
                            t_lastscore[h] = bank.used
                        blk.tensor(f)

                        _t_used = bank.used
                        _t_pb = (t_pvmm.get((h, q4, i - 4)) if i >= 4 else None) or \
                            t_pv.get((h, q4 - 1)) or t_pv.get((h - 1, 3))

                        def f(act, i=i, kt=kt, bank=bank, _t_used=_t_used, _t_pb=_t_pb):
                            wt(act, _t_used)
                            wt(act, _t_pb)
                            if kt == "glob":
                                return act.activation(p_g[:, :], bank.p[0:16, :], AF.Exp)
                            return act.activation(p_sb[:, (i % 4) * 512:(i % 4 + 1) * 512],
                                                  bank.p[:, :], AF.Exp)
                        bank.freed = da_emit(f)
                        t_exp[(h, q4, i)] = bank.freed

                        if i >= 1:
                            emit_pv(h, q4, i - 1, klist, hh, t_exp, t_pvmm, t_pv)
                    emit_pv(h, q4, nk - 1, klist, hh, t_exp, t_pvmm, t_pv)
                    t_pv[(h, q4)] = s_pe.tok
                    while pending_tail:
                        pending_tail.pop(0)()

                    def f(dve, h=h, q4=q4):
                        sl = slice(q4 * 512, (q4 + 1) * 512)
                        wt(dve, t_pv[(h, q4)])
                        dve.wait_ge(s_dve.h, s_dve.n)
                        s_dve.inc(dve.tensor_copy(rowA[:, sl], b_sr.p[0:1, :]))
                    blk.vector(f)
                    b_sr.freed = s_dve.tok

                    def f(dve, h=h, q4=q4):
                        sl = slice(q4 * 512, (q4 + 1) * 512)
                        dve.wait_ge(s_dve.h, s_dve.n)
                        s_dve.inc(dve.reciprocal(rowB[:, sl], rowA[:, sl]))
                        dve.wait_ge(s_dve.h, s_dve.n)
                        s_dve.inc(dve.tensor_copy(rowH[:, sl], rowB[:, sl]))
                        dve.wait_ge(s_dve.h, s_dve.n)
                        s_dve.inc(dve.tensor_tensor(rowC[:, sl], rowB[:, sl], rowH[:, sl],
                                                    ALU.subtract))
                        dve.wait_ge(s_dve.h, s_dve.n)
                        s_dve.inc(dve.tensor_copy(rowL[:, sl], rowC[:, sl]))
                    blk.vector(f)
                    t_rhl4 = s_dve.tok

                    def tail_fin(h=h, q4=q4, t_rhl4=t_rhl4):
                        def f(t):
                            wt(t, t_rhl4)
                            wt(t, b_rb.freed)
                            t.matmul(b_rb.p[:, :], ones_row[:, :], rowH[:, q4 * 512:(q4 + 1) * 512],
                                     start=True, stop=False)
                            ins = t.matmul(b_rb.p[:, :], ones_row[:, :], rowL[:, q4 * 512:(q4 + 1) * 512],
                                           start=False, stop=True)
                            b_rb.used = s_pe.inc(ins)
                        blk.tensor(f)

                        def f(dve):
                            wt(dve, b_rb.used)
                            wt(dve, t_ao.get((h, q4 - 1)) or t_ao.get((h - 1, 3)))
                            dve.wait_ge(s_dve.h, s_dve.n)
                            b_rb.freed = s_dve.inc(dve.tensor_copy(rb_sb[:, :], b_rb.p[:, :]))
                        blk.vector(f)
                        t_rb = s_dve.tok

                        def f(dve, t_rb=t_rb):
                            wt(dve, t_rb)
                            idx = (h * 4 + q4) % 2
                            wt(dve, t_ao_st.get((h * 4 + q4 - 2)))
                            ins = dve.tensor_tensor(aostg[:, idx * 512:(idx + 1) * 512],
                                                    b_pv[q4 % 2].p[:, :], rb_sb[:, :], ALU.mult)
                            s_dve.inc(ins)
                            b_pv[q4 % 2].freed = s_dve.tok
                        blk.vector(f)
                        t_ao[(h, q4)] = s_dve.tok

                        def f(sync, t_aoe=s_dve.tok):
                            wt(sync, t_aoe)
                            idx = (h * 4 + q4) % 2
                            dst = ao_dram[4 * q4:4 * q4 + 4, h].rearrange("j p k -> p j k")
                            srcv = aostg[:, idx * 512:(idx + 1) * 512].rearrange(
                                "p (j k) -> p j k", j=4)
                            s_ao[idx].inc(sync.dma_start(out=dst, in_=srcv), 16)
                        blk.sync(f)
                        t_ao_st[h * 4 + q4] = s_ao[(h * 4 + q4) % 2].tok

                    pending_tail.append(tail_fin)

        while pending_tail:
            pending_tail.pop(0)()
        barrier()
        pp.close()
        p2i.close()
        p1s.close()
        p2s.close()

        p3s = ExitStack()
        v1h = sba(p3s, "v1ht", [128, 8 * D], F32)
        woT = sba(p3s, "woTt", [128, 2 * ND * 1024], BF16)
        aot = sba(p3s, "aott", [128, 2 * D], BF16)
        xres = sba(p3s, "xrest", [128, 2 * D], F32)
        p3p = ExitStack()
        b_wo = [Bank(psa(p3p, "p_wo0", [128, 512])), Bank(psa(p3p, "p_wo1", [128, 512]))]

        t_spill_half = {}
        t_wold = {}
        t_womm = {}
        t_woev = {}
        t_aotld = {}
        t_ln2 = {}
        t_spill = {}
        t_xres = {}

        def f(pool):
            for sl in range(4):
                base = sl * ND * 512
                for ch in range(4):
                    c0 = ch * ND * 128
                    s_wos[sl].inc(pool.dma_start(
                        out=woT[:, base + c0:base + c0 + ND * 128],
                        in_=wo_in[sl][:, c0:c0 + ND * 128]), 16)
        blk.gpsimd(f)
        t_wosl = [s_wos[sl].tok for sl in range(4)]

        for th in range(2):
            for eh in range(2):

                for ti in range(8):
                    tt = th * 8 + ti
                    abuf = ti % 2

                    def f(sync, th=th, eh=eh, ti=ti, tt=tt, abuf=abuf):
                        prev = t_womm.get((th, eh, ti - 2)) or t_womm.get((th, eh - 1, 6 + ti)) \
                            or t_womm.get((th - 1, 1, 6 + ti))
                        wt(sync, prev)
                        src_ = ao_dram[tt].rearrange("h p k -> p h k")
                        dstv = aot[:, abuf * D:(abuf + 1) * D].rearrange(
                            "p (h k) -> p h k", h=NH)
                        s_aot[abuf].inc(sync.dma_start(out=dstv, in_=src_), 16)
                    blk.sync(f)
                    t_aotld[(th, eh, ti)] = s_aot[abuf].tok

                    for e2 in range(2):
                        e4 = 2 * eh + e2
                        bank = b_wo[e2]

                        def f(t, th=th, eh=eh, ti=ti, e2=e2, e4=e4, abuf=abuf, bank=bank):
                            if th == 0 and ti == 0:
                                wt(t, t_wosl[e4])
                            if e2 == 0:
                                wt(t, t_aotld[(th, eh, ti)])
                            wt(t, bank.freed)
                            off = eh * ND * 1024 + e2 * ND * 512
                            for hc in range(NH):
                                ins = t.matmul(bank.p[:, :],
                                               aot[:, abuf * D + hc * 128:abuf * D + (hc + 1) * 128],
                                               woT[:, off + hc * 512:off + (hc + 1) * 512],
                                               start=(hc == 0), stop=(hc == NH - 1))
                            bank.used = s_pe.inc(ins)
                            t_womm[(th, eh, ti)] = bank.used
                        blk.tensor(f)

                        def f(dve, th=th, eh=eh, ti=ti, e2=e2, e4=e4, bank=bank):
                            wt(dve, bank.used)
                            if eh == 0 and e2 == 0 and th >= 1:
                                wt(dve, t_spill_half.get(th - 1))
                            dve.wait_ge(s_dve.h, s_dve.n)
                            ins = dve.tensor_copy(v1h[:, ti * D + e4 * 512:ti * D + (e4 + 1) * 512],
                                                  bank.p[:, :])
                            bank.freed = s_dve.inc(ins)
                        blk.vector(f)
                        t_woev[(th, eh, ti, e2)] = s_dve.tok
                t_womm[(th, eh)] = s_pe.tok

            for ti in range(8):
                tt = th * 8 + ti
                buf = ti % 2

                def f(sync, th=th, ti=ti, tt=tt, buf=buf):
                    wt(sync, t_ln2.get((th, ti - 2)) or t_ln2.get((th - 1, 6 + ti)))
                    s_xr[buf].inc(sync.dma_start(out=xres[:, buf * D:(buf + 1) * D],
                                                 in_=x_in[tt * 128:(tt + 1) * 128, :]), 16)
                blk.sync(f)
                t_xres[(th, ti)] = s_xr[buf].tok

                def f(dve, th=th, ti=ti, buf=buf):
                    wt(dve, t_xres[(th, ti)])
                    wt(dve, t_woev[(th, 1, ti, 1)])
                    vv = v1h[:, ti * D:(ti + 1) * D]
                    dve.wait_ge(s_dve.h, s_dve.n)
                    s_dve.inc(dve.scalar_tensor_tensor(vv, xres[:, buf * D:(buf + 1) * D],
                                                       2.0, vv, ALU.mult, ALU.add))
                blk.vector(f)
                t_v1 = s_dve.tok

                t_rstd = emit_ln_stats(lambda ti=ti: v1h[:, ti * D:(ti + 1) * D], t_v1)

                def f(dve, th=th, ti=ti, t_rstd=t_rstd):
                    wt(dve, t_rstd)
                    vv = v1h[:, ti * D:(ti + 1) * D]
                    dve.wait_ge(s_dve.h, s_dve.n)
                    s_dve.inc(dve.tensor_scalar(vv, vv, agm[:, 0:1], rstd[:, 0:1],
                                                ALU.subtract, ALU.mult))
                blk.vector(f)
                t_ln2[(th, ti)] = s_dve.tok

                def f(sync, tt=tt, ti=ti, t_r2=s_dve.tok):
                    wt(sync, t_r2)
                    s_r2s.inc(sync.dma_start(out=r2_dram[tt * 128:(tt + 1) * 128, :],
                                              in_=v1h[:, ti * D:(ti + 1) * D]), 16)
                blk.sync(f)
                t_spill[(th, ti)] = s_r2s.tok
            t_spill_half[th] = s_r2s.tok

        barrier()
        p3p.close()
        p3s.close()

        p4s = ExitStack()
        hmidt = sba(p4s, "hmidt", [128, NHB * 1024], BF16)
        hmid = hmidt

        t_gu = {}
        t_hmid = {}
        t_hmid_all = {}
        t_dpe = {}
        t_ln3 = {}
        t_outst = {}
        t_v2ld = {}
        t_v2spst = {}
        t_cshl = {}
        t_hcshl = {}
        t_trall = {}
        t_trtile = {}
        t_r2rd = {}
        reb = {}

        def emit_rebuild_loads(p):
            pxs = ExitStack()
            xn2h = sba(pxs, "xn2ht%d" % p, [128, ND * 1024], BF16, is_transpose_dest=True)
            prb = ExitStack()
            n2p = sba(prb, "n2pt%d" % p, [128, 4 * D], BF16)
            reb[p] = dict(pxs=pxs, prb=prb, xn2h=xn2h, n2p=n2p, t_n2p={})

            def f(pool, p=p, n2p=n2p):
                if p > 0:
                    wt(pool, t_cshl[p - 1])
                for ti in range(4):
                    tt = p * 8 + ti
                    s_n2l[ti].inc(pool.dma_start(out=n2p[:, ti * D:(ti + 1) * D],
                                                 in_=r2_dram[tt * 128:(tt + 1) * 128, :]), 16)
            blk.gpsimd(f)
            for ti in range(4):
                reb[p]["t_n2p"][ti] = s_n2l[ti].tok

        def emit_late_load(p, ti):
            st = reb[p]

            def f(pool, p=p, ti=ti, n2p=st["n2p"]):
                wt(pool, t_trtile[(p, ti - 4)])
                tt = p * 8 + ti
                s_n2l[ti % 4].inc(pool.dma_start(out=n2p[:, (ti % 4) * D:(ti % 4 + 1) * D],
                                                 in_=r2_dram[tt * 128:(tt + 1) * 128, :]), 16)
            blk.gpsimd(f)
            st["t_n2p"][ti] = s_n2l[ti % 4].tok

        def emit_transpose_tile(p, ti):
            st = reb[p]

            def f(sync, p=p, ti=ti, xn2h=st["xn2h"], n2p=st["n2p"], t_ld=st["t_n2p"][ti]):
                wt(sync, t_ld)
                if ti == 0 and p > 0:
                    wt(sync, t_gu[(p - 1, NHB - 1)])
                    wt(sync, t_hmid[(p - 1, NHB - 1, 1)])
                for c in range(ND):
                    ins = sync.dma_start_transpose(
                        out=xn2h[:, c * 1024 + ti * 128:c * 1024 + (ti + 1) * 128],
                        in_=n2p[:, (ti % 4) * D + c * 128:(ti % 4) * D + (c + 1) * 128])
                    s_tr[ti].inc(ins, 16)
            blk.sync(f)
            t_trtile[(p, ti)] = s_tr[ti].tok

        def emit_colsum(p):
            xn2h = reb[p]["xn2h"]
            t_csev = {}
            for tc in range(2):
                def f(t, p=p, tc=tc, xn2h=xn2h):
                    if tc == 0:
                        for tok in t_trall[p]:
                            wt(t, tok)
                    wt(t, b_cs.freed)
                    for dc in range(ND):
                        ins = t.matmul(b_cs.p[0:1, :], ones_col[:, :],
                                       xn2h[:, dc * 1024 + tc * 512:dc * 1024 + (tc + 1) * 512],
                                       start=(dc == 0), stop=(dc == ND - 1))
                    b_cs.used = s_pe.inc(ins)
                blk.tensor(f)

                def f(dve, tc=tc):
                    wt(dve, b_cs.used)
                    dve.wait_ge(s_dve.h, s_dve.n)
                    b_cs.freed = s_dve.inc(dve.tensor_copy(rowA[:, tc * 512:(tc + 1) * 512],
                                                           b_cs.p[0:1, :]))
                blk.vector(f)
                t_csev[tc] = s_dve.tok

            def f(dve, p=p):
                wt(dve, t_csev[1])
                if p > 0:
                    wt(dve, t_dpe[(p - 1, 3)])
                dve.wait_ge(s_dve.h, s_dve.n)
                s_dve.inc(dve.tensor_copy(rowH[:, 0:1024], rowA[:, 0:1024]))
                dve.wait_ge(s_dve.h, s_dve.n)
                s_dve.inc(dve.tensor_tensor(rowC[:, 0:1024], rowA[:, 0:1024],
                                            rowH[:, 0:1024], ALU.subtract))
                dve.wait_ge(s_dve.h, s_dve.n)
                s_dve.inc(dve.tensor_copy(rowL[:, 0:1024], rowC[:, 0:1024]))
            blk.vector(f)
            t_cshl[p] = s_dve.tok

        def emit_gate_up(p, ln3_hook=None):
            xn2h = reb[p]["xn2h"]
            pgu = ExitStack()
            wgT = sba(pgu, "wgTt%d" % p, [128, 2 * D], BF16)
            wuT = sba(pgu, "wuTt%d" % p, [128, 2 * D], BF16)
            gus = sba(pgu, "gust%d" % p, [128, 1024], F32)
            t_wgld = {}
            t_gact = {}
            for hb in range(NHB):
                j = hb % 2

                def f(pool, p=p, hb=hb, j=j, wgT=wgT, wuT=wuT):
                    prev = t_gu.get((p, hb - 2)) or t_gu.get((p - 1, NHB - 2 + hb))
                    wt(pool, prev)
                    if hb < 2:
                        for tok in t_trall[p]:
                            wt(pool, tok)
                    s_wg[j].inc(pool.dma_start(out=wgT[:, j * D:(j + 1) * D], in_=wg_in[hb]), 16)
                    s_wg[j].inc(pool.dma_start(out=wuT[:, j * D:(j + 1) * D], in_=wu_in[hb]), 16)
                blk.gpsimd(f)
                t_wgld[hb] = s_wg[j].tok

                for tc in range(2):
                    for gu in range(2):
                        bank = b_gu[tc * 2 + gu]

                        def f(t, p=p, hb=hb, j=j, tc=tc, gu=gu, bank=bank,
                              wgT=wgT, wuT=wuT, xn2h=xn2h):
                            if tc == 0 and gu == 0:
                                wt(t, t_wgld[hb])
                                wt(t, t_cshl[p])
                            wt(t, bank.freed)
                            wsl = (wgT if gu == 0 else wuT)
                            for dc in range(ND):
                                t.matmul(bank.p[:, :], wsl[:, j * D + dc * 128:j * D + (dc + 1) * 128],
                                         xn2h[:, dc * 1024 + tc * 512:dc * 1024 + (tc + 1) * 512],
                                         start=(dc == 0), stop=False)
                            t.matmul(bank.p[:, :], neg128[:, 0:128],
                                     rowH[:, tc * 512:(tc + 1) * 512], start=False, stop=False)
                            ins = t.matmul(bank.p[:, :], neg128[:, 0:128],
                                           rowL[:, tc * 512:(tc + 1) * 512], start=False, stop=True)
                            bank.used = s_pe.inc(ins)
                        blk.tensor(f)
                    t_gu[(p, hb)] = s_pe.tok

                for tc in range(2):
                    bg = b_gu[tc * 2]
                    bu = b_gu[tc * 2 + 1]

                    _prev = t_hmid.get((p, hb, tc - 1)) if tc == 1 else \
                        (t_hmid.get((p, hb - 1, 1)) or t_hmid.get((p - 1, NHB - 1, 1)))

                    def f(act, hb=hb, bg=bg, gus=gus, _t_used=bg.used, _prev=_prev):
                        wt(act, _t_used)
                        wt(act, _prev)
                        return act.activation(gus[:, 0:512], bg.p[:, :], AF.Silu,
                                              scale=sgt[:, hb:hb + 1])
                    bg.freed = da_emit(f)
                    t_gact[(hb, tc)] = bg.freed

                    def f(dve, p=p, hb=hb, tc=tc, bu=bu, gus=gus):
                        wt(dve, bu.used)
                        if hb == 0 and tc == 0:
                            wt(dve, t_dpe.get((p - 1, 3)))
                        dve.wait_ge(s_dve.h, s_dve.n)
                        ins = dve.tensor_scalar(gus[:, 512:1024], bu.p[:, :],
                                                sut[:, hb:hb + 1], None, ALU.mult)
                        bu.freed = s_dve.inc(ins)
                        wt(dve, t_gact[(hb, tc)])
                        dve.wait_ge(s_dve.h, s_dve.n)
                        ins = dve.tensor_tensor(
                            hmid[:, hb * 1024 + tc * 512:hb * 1024 + (tc + 1) * 512],
                            gus[:, 0:512], gus[:, 512:1024], ALU.mult)
                        s_dve.inc(ins)
                    blk.vector(f)
                    t_hmid[(p, hb, tc)] = s_dve.tok

                if ln3_hook is not None and hb >= 2 and hb <= 30 and hb % 4 == 2:
                    ln3_hook((hb - 2) // 4)
            t_hmid_all[p] = s_dve.tok
            pgu.close()

        def emit_hcolsum(p):
            t_hcs_ev = {}
            for tc in range(2):
                def f(t, p=p, tc=tc):
                    wt(t, t_hmid_all[p])
                    wt(t, b_cs.freed)
                    for hb in range(NHB):
                        ins = t.matmul(b_cs.p[0:1, :], ones_col[:, :],
                                       hmid[:, hb * 1024 + tc * 512:hb * 1024 + (tc + 1) * 512],
                                       start=(hb == 0), stop=(hb == NHB - 1))
                    b_cs.used = s_pe.inc(ins)
                blk.tensor(f)

                def f(dve, tc=tc):
                    wt(dve, b_cs.used)
                    dve.wait_ge(s_dve.h, s_dve.n)
                    b_cs.freed = s_dve.inc(dve.tensor_copy(rowA[:, tc * 512:(tc + 1) * 512],
                                                           b_cs.p[0:1, :]))
                blk.vector(f)
                t_hcs_ev[tc] = s_dve.tok

            def f(dve):
                wt(dve, t_hcs_ev[1])
                dve.wait_ge(s_dve.h, s_dve.n)
                s_dve.inc(dve.tensor_copy(rowH[:, 0:1024], rowA[:, 0:1024]))
                dve.wait_ge(s_dve.h, s_dve.n)
                s_dve.inc(dve.tensor_tensor(rowC[:, 0:1024], rowA[:, 0:1024],
                                            rowH[:, 0:1024], ALU.subtract))
                dve.wait_ge(s_dve.h, s_dve.n)
                s_dve.inc(dve.tensor_copy(rowL[:, 0:1024], rowC[:, 0:1024]))
            blk.vector(f)
            t_hcshl[p] = s_dve.tok

        def emit_down(p, wdT, v2sp, tr_hook=None, ld_hook=None):
            t_wdld = {}
            for e4 in range(4):
                for hb in range(NHB):
                    j = hb % 2

                    def f(pool, p=p, e4=e4, hb=hb, j=j, wdT=wdT):
                        prev = t_dpe.get((p, e4, hb - 2)) or t_dpe.get((p, e4 - 1, NHB - 2 + hb)) \
                            or t_dpe.get((p - 1, 3, NHB - 2 + hb))
                        wt(pool, prev)
                        if e4 == 0 and hb < 2:
                            wt(pool, t_gu[(p, NHB - 1)])
                            wt(pool, t_hmid[(p, NHB - 1, 1)])
                            if p > 0:
                                wt(pool, t_outst[(p - 1, 7)])
                        s_wd[j].inc(pool.dma_start(out=wdT[:, j * 512:(j + 1) * 512],
                                                   in_=wd_in[e4, hb * 128:(hb + 1) * 128, :]), 16)
                    blk.gpsimd(f)
                    t_wdld[(e4, hb)] = s_wd[j].tok
                    if ld_hook is not None and e4 == 0 and hb >= 8 and hb % 4 == 0 and hb < 24:
                        ld_hook(4 + (hb - 8) // 4)

                    def f(t, p=p, e4=e4, hb=hb, j=j, wdT=wdT):
                        wt(t, t_wdld[(e4, hb)])
                        if hb == 0:
                            wt(t, t_hcshl[p])
                            for ti in range(8):
                                wt(t, pd_banks[ti].freed)
                        for ti in range(8):
                            ins = t.matmul(pd_banks[ti].p[:, :],
                                           hmid[:, hb * 1024 + ti * 128:hb * 1024 + (ti + 1) * 128],
                                           wdT[:, j * 512:(j + 1) * 512],
                                           start=(hb == 0), stop=False)
                        if hb == NHB - 1:
                            for ti in range(8):
                                t.matmul(pd_banks[ti].p[:, :],
                                         rowH[:, ti * 128:(ti + 1) * 128], neg128[:, :],
                                         start=False, stop=False)
                                ins = t.matmul(pd_banks[ti].p[:, :],
                                               rowL[:, ti * 128:(ti + 1) * 128], neg128[:, :],
                                               start=False, stop=True)
                                pd_banks[ti].used = s_pe.inc(ins)
                        else:
                            s_pe.inc(ins)
                        t_dpe[(p, e4, hb)] = s_pe.tok
                    blk.tensor(f)
                t_dpe[(p, e4)] = s_pe.tok

                for ti in range(8):
                    bank = pd_banks[ti]

                    def f(dve, p=p, e4=e4, ti=ti, bank=bank, v2sp=v2sp):
                        wt(dve, bank.used)
                        dve.wait_ge(s_dve.h, s_dve.n)
                        wt(dve, s_v2s[ti % 2].tok)
                        dst = v2sp[:, (ti % 2) * 512:(ti % 2 + 1) * 512]
                        ins = dve.tensor_tensor(dst, bank.p[:, :],
                                                sd_bc[:, e4 * 512:(e4 + 1) * 512], ALU.mult)
                        bank.freed = s_dve.inc(ins)
                    blk.vector(f)

                    def f(sync, p=p, e4=e4, ti=ti, t_e=s_dve.tok, v2sp=v2sp):
                        wt(sync, t_e)
                        s_v2s[ti % 2].inc(sync.dma_start(
                            out=v2_dram[p, e4, ti],
                            in_=v2sp[:, (ti % 2) * 512:(ti % 2 + 1) * 512]), 16)
                    blk.sync(f)
                    t_v2spst[(p, e4, ti)] = s_v2s[ti % 2].tok
                    if tr_hook is not None and e4 == 1 and ti % 2 == 1:
                        tr_hook(4 + ti // 2)

        def emit_ln3_tile(p, ti, v2a, r2tl, nsl):
            tt = p * 8 + ti
            buf = ti % nsl

            def f(sync, p=p, ti=ti, tt=tt, buf=buf, v2a=v2a, r2tl=r2tl):
                wt(sync, t_v2spst[(p, 3, ti)])
                wt(sync, t_outst.get((p, ti - 1)) or t_outst.get((p - 1, 7)))
                src = v2_dram[p, :, ti].rearrange("e p k -> p e k")
                dstv = v2a[:, :].rearrange("p (e k) -> p e k", e=4)
                s_v2l.inc(sync.dma_start(out=dstv, in_=src), 16)
                wt(sync, t_r2rd.get((p, ti - nsl)) or (t_ln3.get((p - 1, 7)) if p > 0 else None))
                s_r2l[buf].inc(sync.dma_start(out=r2tl[:, buf * D:(buf + 1) * D],
                                              in_=r2_dram[tt * 128:(tt + 1) * 128, :]), 16)
            blk.sync(f)
            t_v2ld[(p, ti)] = (s_v2l, s_v2l.n)
            t_r2ld3 = s_r2l[buf].tok

            def f(dve, p=p, ti=ti, buf=buf, t_r2ld3=t_r2ld3, v2a=v2a, r2tl=r2tl):
                wt(dve, t_v2ld[(p, ti)])
                wt(dve, t_r2ld3)
                dve.wait_ge(s_dve.h, s_dve.n)
                s_dve.inc(dve.tensor_tensor(v2a[:, :], v2a[:, :],
                                            r2tl[:, buf * D:(buf + 1) * D], ALU.add))
            blk.vector(f)
            t_v2 = s_dve.tok
            t_r2rd[(p, ti)] = t_v2

            t_rstd = emit_ln_stats(lambda v2a=v2a: v2a[:, :], t_v2)

            def f(dve, p=p, ti=ti, t_rstd=t_rstd, v2a=v2a):
                wt(dve, t_rstd)
                dve.wait_ge(s_dve.h, s_dve.n)
                s_dve.inc(dve.tensor_scalar(v2a[:, :], v2a[:, :], agm[:, 0:1], rstd[:, 0:1],
                                            ALU.subtract, ALU.mult))
            blk.vector(f)
            t_ln3[(p, ti)] = s_dve.tok

            def f(sync, tt=tt, t_o=s_dve.tok, v2a=v2a):
                wt(sync, t_o)
                s_out.inc(sync.dma_start(out=out_dram[tt * 128:(tt + 1) * 128, :],
                                          in_=v2a[:, :]), 16)
            blk.sync(f)
            t_outst[(p, ti)] = s_out.tok

        emit_rebuild_loads(0)
        ptr = ExitStack()
        trb = [Bank(ptr.enter_context(nc.psum_tensor("p_trz%d" % i, [128, 1024], BF16)))
               for i in range(6)]
        for ti in range(8):
            if ti >= 4:
                emit_late_load(0, ti)
            st0 = reb[0]
            for c in range(ND):
                bank = trb[(ti * ND + c) % 6]

                def f(t, ti=ti, c=c, bank=bank, n2p=st0["n2p"], t_ld=st0["t_n2p"][ti]):
                    if c == 0:
                        wt(t, t_ld)
                    wt(t, bank.freed)
                    ins = t.transpose(bank.p[:, 0:128],
                                      n2p[:, (ti % 4) * D + c * 128:(ti % 4) * D + (c + 1) * 128],
                                      ident[:, :])
                    bank.used = s_pe.inc(ins)
                blk.tensor(f)

                def f(dve, ti=ti, c=c, bank=bank, xn2h=st0["xn2h"]):
                    wt(dve, bank.used)
                    dve.wait_ge(s_dve.h, s_dve.n)
                    ins = dve.tensor_copy(xn2h[:, c * 1024 + ti * 128:c * 1024 + (ti + 1) * 128],
                                          bank.p[:, 0:128])
                    bank.freed = s_dve.inc(ins)
                blk.vector(f)
            t_trtile[(0, ti)] = s_pe.tok
        ptr.close()
        t_trall[0] = [s_pe.tok, s_dve.tok]
        p4p = ExitStack()
        b_gu = [Bank(psa(p4p, "p_gu%d" % i, [128, 512])) for i in range(4)]
        b_cs = Bank(psa(p4p, "p_cs2", [128, 512]))
        b_s2 = [Bank(psa(p4p, "p_s20", [128, 512])), Bank(psa(p4p, "p_s21", [128, 512]))]
        b_x = Bank(psa(p4p, "p_x2", [128, 512]))
        pd_banks = b_gu + b_s2 + [b_cs, b_x]
        emit_colsum(0)
        reb[0]["prb"].close()
        emit_gate_up(0)
        reb[0]["pxs"].close()
        emit_hcolsum(0)

        pdn0 = ExitStack()
        wdT0 = sba(pdn0, "wdTt0", [128, 2 * 512], BF16)
        v2sp0 = sba(pdn0, "v2spt0", [128, 2 * 512], F32)
        v2a0 = sba(pdn0, "v2at0", [128, D], F32)
        r2t0 = sba(pdn0, "r2t0t", [128, D], F32)
        emit_rebuild_loads(1)
        for ti in range(4):
            emit_transpose_tile(1, ti)
        emit_down(0, wdT0, v2sp0, tr_hook=lambda k: emit_transpose_tile(1, k),
                  ld_hook=lambda k: emit_late_load(1, k))
        t_trall[1] = [s_tr[i].tok for i in range(8)]

        emit_colsum(1)
        reb[1]["prb"].close()
        emit_gate_up(1, ln3_hook=lambda ti: emit_ln3_tile(0, ti, v2a0, r2t0, 1))
        reb[1]["pxs"].close()
        pdn0.close()
        emit_hcolsum(1)

        pdn1 = ExitStack()
        wdT1 = sba(pdn1, "wdTt1", [128, 2 * 512], BF16)
        v2sp1 = sba(pdn1, "v2spt1", [128, 2 * 512], F32)
        v2a1 = sba(pdn1, "v2at1", [128, D], F32)
        r2t1 = sba(pdn1, "r2t1t", [128, 2 * D], F32)
        emit_down(1, wdT1, v2sp1)
        for ti in range(8):
            emit_ln3_tile(1, ti, v2a1, r2t1, 2)
        pdn1.close()
        p4p.close()
        p4s.close()

        flush_act()

        def f(sync):
            sync.wait_ge(s_out.h, s_out.n)
        blk.sync(f)

    outer.close()
    return nc


def emit_pv(h, q4, i, klist, hh, t_exp, t_pvmm, t_pv):
    blk = emit_pv.blk
    s_pe = emit_pv.s_pe
    b_pv = emit_pv.b_pv
    b_sr = emit_pv.b_sr
    v4 = emit_pv.v4
    p_sb = emit_pv.p_sb
    p_g = emit_pv.p_g
    ones_col = emit_pv.ones_col
    kt = klist[i]
    nk = len(klist)
    bpv = b_pv[q4 % 2]

    def f(t):
        wt(t, t_exp[(h, q4, i)])
        if i == 0:
            wt(t, bpv.freed)
            wt(t, b_sr.freed)
        first = (i == 0)
        last = (i == nk - 1)
        if kt == "glob":
            t.matmul(bpv.p[:, :], v4[0:16, 16 * 512 + hh * 128:16 * 512 + (hh + 1) * 128],
                     p_g[:, :], start=first, stop=last)
            ins = t.matmul(b_sr.p[0:1, :], ones_col[0:16, :], p_g[:, :],
                           start=first, stop=last)
        else:
            t.matmul(bpv.p[:, :], v4[:, kt * 512 + hh * 128:kt * 512 + (hh + 1) * 128],
                     p_sb[:, (i % 4) * 512:(i % 4 + 1) * 512], start=first, stop=last)
            ins = t.matmul(b_sr.p[0:1, :], ones_col[:, :],
                           p_sb[:, (i % 4) * 512:(i % 4 + 1) * 512], start=first, stop=last)
        tok = s_pe.inc(ins)
        t_pvmm[(h, q4, i)] = tok
        if last:
            bpv.used = tok
            b_sr.used = tok
    blk.tensor(f)


def host_prepare(L0_attn_x, L0_attn_global_tokens, L0_attn_ln_scale, L0_attn_ln_bias,
                 L0_attn_Wq, L0_attn_Wk, L0_attn_Wv, L0_attn_Wo,
                 L0_ffn_wq_gate, L0_ffn_s_gate, L0_ffn_wq_up, L0_ffn_s_up,
                 L0_ffn_wq_down, L0_ffn_s_down):
    ac = np.ascontiguousarray
    x = np.asarray(L0_attn_x, dtype=np.float32)
    g = ac(np.asarray(L0_attn_global_tokens, dtype=np.float32).reshape(G, D))
    lnw = ac(np.asarray(L0_attn_ln_scale, np.float32).reshape(ND, 128).T)
    lnb = ac(np.asarray(L0_attn_ln_bias, np.float32).reshape(ND, 128).T)
    def warr(w, nslice, csz):
        return ac(np.asarray(w, np.float32).T.reshape(ND, 128, nslice, csz)
                  .transpose(2, 1, 0, 3).reshape(nslice, 128, ND * csz))

    def warr_i(w, nslice, csz):
        return ac(np.asarray(w, np.int32).T.reshape(ND, 128, nslice, csz)
                  .transpose(2, 1, 0, 3).reshape(nslice, 128, ND * csz).astype(np.int32))

    wq = warr(L0_attn_Wq, NH, HD)
    wk = warr(L0_attn_Wk, NH, HD)
    wv = warr(L0_attn_Wv, 4, 512)
    wo = warr(L0_attn_Wo, 4, 512)
    wg = warr_i(L0_ffn_wq_gate, NHB, 128)
    wu = warr_i(L0_ffn_wq_up, NHB, 128)
    wd = ac(np.asarray(L0_ffn_wq_down, np.int32).T.reshape(HID, 4, 512).transpose(1, 0, 2))
    sg = ac(np.asarray(L0_ffn_s_gate, np.float32).reshape(NHB, 128).T)
    su = ac(np.asarray(L0_ffn_s_up, np.float32).reshape(NHB, 128).T)
    sd = ac(np.asarray(L0_ffn_s_down, np.float32).reshape(1, D))

    ident = np.eye(128, dtype=bfdt)
    ones_col = np.ones((128, 1), dtype=bfdt)
    ones_row = np.ones((1, 128), dtype=bfdt)
    neg128 = np.full((1, 512), -128.0, dtype=bfdt)
    pidx = np.arange(128)[:, None]
    fidx = np.arange(512)[None, :]
    m6 = np.empty((128, 6 * 512), dtype=bfdt)
    for iv in range(6):
        v = iv - 2
        diff = fidx - v * 128 - pidx
        valid = (diff >= 0) & (diff <= WIN - 1)
        m6[:, iv * 512:(iv + 1) * 512] = np.where(valid, 0.0, NEG).astype(bfdt)

    shared = dict(g=g, lnw=lnw, lnb=lnb, wq=wq, wk=wk, wv=wv, wo=wo,
                  wg=wg, wu=wu, wd=wd, sgv=sg, suv=su, sdv=sd,
                  ident=ident, mask6=m6, ones_col=ones_col, ones_row=ones_row,
                  neg128=neg128)
    in_maps = []
    for core in range(8):
        m = dict(shared)
        m["x"] = ac(x[core])
        in_maps.append(m)
    return in_maps


_NC_CACHE = {}


def kernel(**inputs):
    in_maps = host_prepare(**inputs)
    if "nc" not in _NC_CACHE:
        _NC_CACHE["nc"] = build_nc()
    nc = _NC_CACHE["nc"]
    res = run_bass_kernel_spmd(nc, in_maps, core_ids=list(range(8)))
    out = np.stack([np.asarray(res.results[i]["out"]) for i in range(8)], axis=0)
    return out.astype(np.float32)


# version 19
# speedup vs baseline: 1.1725x; 1.0025x over previous
import numpy as np
import ml_dtypes
import concourse.bass as bass
import concourse.mybir as mybir
from concourse.bass_utils import run_bass_kernel_spmd

F32 = mybir.dt.float32
BF16 = mybir.dt.bfloat16
I32 = mybir.dt.int32
AF = mybir.ActivationFunctionType
ALU = mybir.AluOpType
bfdt = ml_dtypes.bfloat16

D = 2048
T = 2048
NH = 16
HD = 128
G = 16
HID = 5632
NHB = HID // 128
WIN = 256
NT = T // 128
ND = D // 128
EPS = 1e-5
QSCALE = float(HD) ** -0.5
NEG = -30000.0


class Sem:
    def __init__(self, nc, name):
        self.g = nc.semaphore(name)
        self.h = self.g.__enter__()
        self.n = 0

    def inc(self, instr, k=1):
        instr.then_inc(self.h, k)
        self.n += k
        return (self, self.n)

    @property
    def tok(self):
        return (self, self.n)


def _wait_only(act, f):
    f(act)
    return None


def wt(eng, token):
    if token is None:
        return
    sem, n = token
    if n > 0:
        eng.wait_ge(sem.h, n)


class Bank:

    def __init__(self, psum):
        self.p = psum
        self.freed = None
        self.used = None


def build_nc(debug=False):
    nc = bass.Bass("TRN2", target_bir_lowering=False)

    x_in = nc.dram_tensor("x", [T, D], F32, kind="ExternalInput").ap()
    g_in = nc.dram_tensor("g", [G, D], F32, kind="ExternalInput").ap()
    lnw_in = nc.dram_tensor("lnw", [128, ND], F32, kind="ExternalInput").ap()
    lnb_in = nc.dram_tensor("lnb", [128, ND], F32, kind="ExternalInput").ap()
    wq_in = nc.dram_tensor("wq", [NH, 128, D], F32, kind="ExternalInput").ap()
    wk_in = nc.dram_tensor("wk", [NH, 128, D], F32, kind="ExternalInput").ap()
    wv_in = nc.dram_tensor("wv", [4, 128, ND * 512], F32, kind="ExternalInput").ap()
    wo_in = nc.dram_tensor("wo", [4, 128, ND * 512], F32, kind="ExternalInput").ap()
    wg_in = nc.dram_tensor("wg", [NHB, 128, D], I32, kind="ExternalInput").ap()
    wu_in = nc.dram_tensor("wu", [NHB, 128, D], I32, kind="ExternalInput").ap()
    wd_in = nc.dram_tensor("wd", [4, HID, 512], I32, kind="ExternalInput").ap()
    sg_in = nc.dram_tensor("sgv", [128, NHB], F32, kind="ExternalInput").ap()
    su_in = nc.dram_tensor("suv", [128, NHB], F32, kind="ExternalInput").ap()
    sd_in = nc.dram_tensor("sdv", [1, D], F32, kind="ExternalInput").ap()
    id_in = nc.dram_tensor("ident", [128, 128], BF16, kind="ExternalInput").ap()
    mk_in = nc.dram_tensor("mask6", [128, 6 * 512], BF16, kind="ExternalInput").ap()
    oc_in = nc.dram_tensor("ones_col", [128, 1], BF16, kind="ExternalInput").ap()
    or_in = nc.dram_tensor("ones_row", [1, 128], BF16, kind="ExternalInput").ap()
    n128_in = nc.dram_tensor("neg128", [1, 512], BF16, kind="ExternalInput").ap()

    out_dram = nc.dram_tensor("out", [T, D], F32, kind="ExternalOutput").ap()
    scratch_kind = "ExternalOutput" if debug else "Internal"
    r2_dram = nc.dram_tensor("r2sp", [T, D], F32, kind=scratch_kind).ap()
    ao_dram = nc.dram_tensor("aosp", [NT, NH, 128, 128], BF16, kind=scratch_kind).ap()
    v2_dram = nc.dram_tensor("v2sp", [2, 4, 8, 128, 512], F32, kind="Internal").ap()
    xn_tap = None
    if debug:
        xn_tap = nc.dram_tensor("xnfmd", [128, ND * T], BF16, kind="ExternalOutput").ap()

    s_pe = Sem(nc, "s_pe")
    s_dve = Sem(nc, "s_dve")
    s_act = Sem(nc, "s_act")
    s_c0 = Sem(nc, "s_c0")
    s_cg = Sem(nc, "s_cg")
    s_cgx = Sem(nc, "s_cgx")
    s_x = [Sem(nc, "s_x0"), Sem(nc, "s_x1")]
    s_xb1 = [Sem(nc, "s_xb1a"), Sem(nc, "s_xb1b")]
    s_wqk = Sem(nc, "s_wqk")
    s_wv = Sem(nc, "s_wv")
    s_ao = [Sem(nc, "s_ao0"), Sem(nc, "s_ao1")]
    s_wo = [Sem(nc, "s_wo0"), Sem(nc, "s_wo1")]
    s_aot = [Sem(nc, "s_aot0"), Sem(nc, "s_aot1")]
    s_xr = [Sem(nc, "s_xr0"), Sem(nc, "s_xr1")]
    s_r2s = Sem(nc, "s_r2s")
    s_r2l = [Sem(nc, "s_r2l0"), Sem(nc, "s_r2l1")]
    s_n2l = [Sem(nc, "s_n2l%d" % i) for i in range(4)]
    s_tr = [Sem(nc, "s_tr%d" % i) for i in range(8)]
    s_wos = [Sem(nc, "s_wos%d" % i) for i in range(4)]
    s_xb2 = [Sem(nc, "s_xb2a"), Sem(nc, "s_xb2b")]
    s_wg = [Sem(nc, "s_wg0"), Sem(nc, "s_wg1")]
    s_wd = [Sem(nc, "s_wd0"), Sem(nc, "s_wd1")]
    s_v2s = [Sem(nc, "s_v2s0"), Sem(nc, "s_v2s1")]
    s_v2l = Sem(nc, "s_v2l")
    s_out = Sem(nc, "s_out")
    all_sems = [s_pe, s_dve, s_act, s_c0, s_cg, s_cgx, s_wqk, s_wv, s_r2s,
                s_v2l, s_out] + s_x + s_xb1 + s_ao + s_wo + s_aot + s_xr +         s_r2l + s_n2l + s_tr + s_wos + s_xb2 + s_wg + s_wd + s_v2s

    from contextlib import ExitStack
    outer = ExitStack()

    def sba(stk, name, shape, dt, **kw):
        return stk.enter_context(nc.sbuf_tensor(name, shape, dt, **kw))

    def psa(stk, name, shape):
        return stk.enter_context(nc.psum_tensor(name, shape, F32))

    def es_psum_bf16(nc_, stk, name):
        return stk.enter_context(nc_.psum_tensor(name, [128, 128], BF16))

    ones_col = sba(outer, "ones_colt", [128, 1], BF16)
    ones_row = sba(outer, "ones_rowt", [1, 128], BF16)
    neg128 = sba(outer, "neg128t", [1, 512], BF16)
    lnw = sba(outer, "lnwt", [128, ND], F32)
    lnb = sba(outer, "lnbt", [128, ND], F32)
    sgt = sba(outer, "sgtt", [128, NHB], F32)
    sut = sba(outer, "sutt", [128, NHB], F32)
    stats = sba(outer, "statst", [128, 24], F32)
    agm = sba(outer, "agt", [128, 2], F32)
    rstd = sba(outer, "rstdt", [128, 1], F32)
    sqv = sba(outer, "sqvt", [128, 1], F32)
    rowA = sba(outer, "rowAt", [1, T], F32)
    rowB = sba(outer, "rowBt", [1, T], F32)
    rowC = sba(outer, "rowCt", [1, T], F32)
    rowH = sba(outer, "rowHt", [1, T], BF16)
    rowL = sba(outer, "rowLt", [1, T], BF16)
    rb_sb = sba(outer, "rb_sbt", [128, 512], F32)
    ident = sba(outer, "identt", [128, 128], BF16)
    epsv = sba(outer, "epsvt", [128, 1], F32)
    sd_bc = sba(outer, "sd_bct", [128, D], F32)

    act_q = []

    def da_emit(fn, inc=1):
        s_act.n += inc
        act_q.append((fn, s_act.n, inc))
        return (s_act, s_act.n)

    with nc.Block() as blk:

        def flush_act():
            q = list(act_q)
            act_q.clear()
            if not q:
                return

            def body(act):
                for fn, n_after, inc in q:
                    ins = fn(act)
                    if inc:
                        ins.then_inc(s_act.h, inc)
            blk.scalar(body)

        def barrier():
            toks = [(s.h, s.n) for s in all_sems]

            def f(eng):
                for h, n in toks:
                    if n > 0:
                        eng.wait_ge(h, n)
            blk.tensor(f)
            blk.vector(f)
            da_emit(lambda act, f=f: f(act) or act.engine_nop() if False else _wait_only(act, f), 0)
            blk.sync(f)
            blk.gpsimd(f)

        p2s = ExitStack()
        masks = sba(p2s, "maskt", [128, 6 * 512], BF16)
        g_bf = sba(p2s, "g_bft", [G, D], BF16)
        g_fmc = sba(p2s, "g_fmct", [128, G * ND], BF16)
        gtr_ev = {}
        sd_row = sba(p2s, "sd_rowt", [1, D], F32)

        def f(sync):
            for dst, src in [(ident, id_in), (masks, mk_in), (ones_col, oc_in),
                             (ones_row, or_in), (neg128, n128_in), (lnw, lnw_in),
                             (lnb, lnb_in), (sgt, sg_in), (sut, su_in),
                             (sd_row, sd_in)]:
                s_c0.inc(sync.dma_start(out=dst[:, :], in_=src[:, :]), 16)
        blk.sync(f)
        t_consts = s_c0.tok

        def f(pool):
            s_cg.inc(pool.dma_start(out=g_bf[:, :], in_=g_in[:, :]), 16)
        blk.gpsimd(f)
        t_gbf = s_cg.tok

        def f(dve):
            s_dve.inc(dve.memset(epsv[:, :], EPS))
        blk.vector(f)

        with nc.psum_tensor("p_gtr", [128, 128], BF16) as p_gtr:
            for c in range(ND):
                def f(t, c=c):
                    if c == 0:
                        wt(t, t_gbf)
                        wt(t, t_consts)
                    else:
                        wt(t, (s_dve, gtr_ev[c - 1]))
                    ins = t.transpose(p_gtr[0:128, 0:16], g_bf[:, c * 128:(c + 1) * 128],
                                      ident[0:16, 0:16])
                    s_pe.inc(ins)
                blk.tensor(f)
                t_m = s_pe.tok

                def f(dve, c=c, t_m=t_m):
                    wt(dve, t_m)
                    dve.wait_ge(s_dve.h, s_dve.n)
                    s_dve.inc(dve.tensor_copy(g_fmc[:, c * G:(c + 1) * G], p_gtr[0:128, 0:16]))
                blk.vector(f)
                gtr_ev[c] = s_dve.n
        t_gfmc = s_dve.tok

        def f(dve):
            wt(dve, t_consts)
            s_dve.inc(dve.tensor_copy(rowH[:, 0:D], sd_row[:, :]))
            dve.wait_ge(s_dve.h, s_dve.n)
            s_dve.inc(dve.tensor_tensor(rowC[:, 0:D], sd_row[:, :], rowH[:, 0:D], ALU.subtract))
            dve.wait_ge(s_dve.h, s_dve.n)
            s_dve.inc(dve.tensor_copy(rowL[:, 0:D], rowC[:, 0:D]))
        blk.vector(f)
        t_sdhl = s_dve.tok

        with nc.psum_tensor("p_sdb", [128, 512], F32) as p_sdb:
            sdb_toks = []
            for j in range(4):
                def f(t, j=j):
                    wt(t, t_sdhl)
                    if sdb_toks:
                        wt(t, sdb_toks[-1])
                    t.matmul(p_sdb[:, :], ones_row[:, :], rowH[:, j * 512:(j + 1) * 512],
                             start=True, stop=False)
                    ins = t.matmul(p_sdb[:, :], ones_row[:, :], rowL[:, j * 512:(j + 1) * 512],
                                   start=False, stop=True)
                    s_pe.inc(ins)
                blk.tensor(f)
                t_mm = s_pe.tok

                def f(dve, j=j, t_mm=t_mm):
                    wt(dve, t_mm)
                    dve.wait_ge(s_dve.h, s_dve.n)
                    s_dve.inc(dve.tensor_copy(sd_bc[:, j * 512:(j + 1) * 512], p_sdb[:, :]))
                blk.vector(f)
                sdb_toks.append(s_dve.tok)

            pass

        p1s = ExitStack()
        xnfm = sba(p1s, "xnfmt", [128, ND * T], BF16)
        p1i = ExitStack()
        xtile = sba(p1i, "xtilet", [128, 2 * D], F32)
        n1 = sba(p1i, "n1t", [128, 2 * D], BF16)
        p1p = ExitStack()
        trb = [Bank(p1p.enter_context(nc.psum_tensor("p_tr%d" % i, [128, 1024], BF16)))
               for i in range(6)]

        n1_tok = {}
        xb_tok = {}
        ln_act_tok = {"prev": None}
        ln_dve_tok = {"prev": None}

        def emit_ln_stats(get_v, tok_ready):
            def f(dve):
                wt(dve, tok_ready)
                wt(dve, ln_act_tok["prev"])
                dve.wait_ge(s_dve.h, s_dve.n)
                vv = get_v()
                for i in range(4):
                    ins = dve.bn_stats(stats[:, i * 6:(i + 1) * 6], vv[:, i * 512:(i + 1) * 512])
                s_dve.inc(ins)
                dve.wait_ge(s_dve.h, s_dve.n)
                s_dve.inc(dve.bn_aggr(agm[:, :], stats[:, :]))
            blk.vector(f)
            t_ag = s_dve.tok

            def f(act, t_ag=t_ag, t_prev=ln_dve_tok["prev"]):
                wt(act, t_ag)
                wt(act, t_prev)
                return act.activation(sqv[:, :], agm[:, 1:2], AF.Sqrt, bias=epsv[:, 0:1])
            t_sq = da_emit(f)
            ln_act_tok["prev"] = t_sq

            def f(dve):
                wt(dve, t_sq)
                dve.wait_ge(s_dve.h, s_dve.n)
                s_dve.inc(dve.reciprocal(rstd[:, :], sqv[:, :]))
            blk.vector(f)
            ln_dve_tok["prev"] = s_dve.tok
            return s_dve.tok

        xt_tok = {}
        for tt in range(NT):
            buf = tt % 2
            hstep = tt % 8

            def f(sync, tt=tt, buf=buf):
                if tt >= 2:
                    wt(sync, n1_tok[tt - 2])
                s_x[buf].inc(sync.dma_start(out=xtile[:, buf * D:(buf + 1) * D],
                                            in_=x_in[tt * 128:(tt + 1) * 128, :]), 16)
            blk.sync(f)
            xt_tok[tt] = s_x[buf].tok

            t_rstd = emit_ln_stats(lambda buf=buf: xtile[:, buf * D:(buf + 1) * D], xt_tok[tt])

            def f(dve, tt=tt, buf=buf, t_rstd=t_rstd):
                wt(dve, t_rstd)
                if tt >= 2:
                    wt(dve, xb_tok[tt - 2])
                s_dve.inc(dve.tensor_scalar(n1[:, buf * D:(buf + 1) * D],
                                            xtile[:, buf * D:(buf + 1) * D],
                                            agm[:, 0:1], rstd[:, 0:1],
                                            ALU.subtract, ALU.mult))
            blk.vector(f)
            n1_tok[tt] = s_dve.tok

            for c in range(ND):
                bank = trb[(tt * ND + c) % 6]

                def f(t, tt=tt, buf=buf, c=c, bank=bank):
                    if c == 0:
                        wt(t, n1_tok[tt])
                        if tt == 0:
                            wt(t, t_consts)
                    wt(t, bank.freed)
                    ins = t.transpose(bank.p[:, 0:128],
                                      n1[:, buf * D + c * 128:buf * D + (c + 1) * 128],
                                      ident[:, :])
                    bank.used = s_pe.inc(ins)
                blk.tensor(f)

                def f(dve, tt=tt, c=c, bank=bank):
                    wt(dve, bank.used)
                    dve.wait_ge(s_dve.h, s_dve.n)
                    ins = dve.tensor_scalar(xnfm[:, c * T + tt * 128:c * T + (tt + 1) * 128],
                                            bank.p[:, 0:128], lnw[:, c:c + 1], lnb[:, c:c + 1],
                                            ALU.mult, ALU.add)
                    bank.freed = s_dve.inc(ins)
                blk.vector(f)
            xb_tok[tt] = s_pe.tok
        t_xnfm = s_dve.tok

        if debug:
            def f(sync):
                wt(sync, t_xnfm)
                s_out.inc(sync.dma_start(out=xn_tap[:, :], in_=xnfm[:, :]), 16)
            blk.sync(f)

        p1p.close()
        p1i.close()
        barrier()

        p2i = ExitStack()
        qb = sba(p2i, "qbt", [128, T], BF16)
        kb = sba(p2i, "kbt", [128, G + T], BF16)
        v4 = sba(p2i, "v4t", [128, 17 * 512], BF16)
        wqkT = sba(p2i, "wqkTt", [128, 2 * D], BF16)
        wvT = sba(p2i, "wvTt", [128, ND * 512], BF16)
        p_sb = sba(p2i, "p_sbt", [128, 4 * 512], BF16)
        p_g = sba(p2i, "p_gt", [16, 512], BF16)
        aostg = sba(p2i, "aostgt", [128, 2 * 512], BF16)

        pp = ExitStack()
        b_sc = [Bank(psa(pp, f"p_sc{i}", [128, 512])) for i in range(4)]
        b_pv = [Bank(psa(pp, f"p_pv{i}", [128, 512])) for i in range(2)]
        b_sr = Bank(psa(pp, "p_srow", [128, 512]))
        b_rb = Bank(psa(pp, "p_rbc", [128, 512]))

        emit_pv.blk = blk
        emit_pv.s_pe = s_pe
        emit_pv.b_pv = b_pv
        emit_pv.b_sr = b_sr
        emit_pv.v4 = v4
        emit_pv.p_sb = p_sb
        emit_pv.p_g = p_g
        emit_pv.ones_col = ones_col

        t_wqk = {}
        t_wv = {}
        t_qkprod = {}
        t_qk_evac = {}
        t_vprod = {}
        t_v_evac = {}
        t_lastscore = {}
        t_pv = {}
        t_ao = {}
        t_ao_st = {}
        t_rbfree = {}

        pending_tail = []

        def kts(q4):
            return list(range(max(0, 4 * q4 - 2), min(NT, 4 * q4 + 4)))

        sc_i = {"i": 0}

        def next_sc():
            b = b_sc[sc_i["i"] % 4]
            sc_i["i"] += 1
            return b

        for hg in range(4):
            def f(pool, hg=hg):
                if hg >= 1:
                    wt(pool, t_vprod[hg - 1])
                s_wv.inc(pool.dma_start(out=wvT[:, :], in_=wv_in[hg]), 16)
            blk.gpsimd(f)
            t_wv[hg] = s_wv.tok

            for h in range(4 * hg, 4 * hg + 4):
                def f(pool, h=h):
                    if h >= 1:
                        wt(pool, t_qkprod[h - 1])
                    s_wqk.inc(pool.dma_start(out=wqkT[:, 0:D], in_=wq_in[h]), 16)
                    s_wqk.inc(pool.dma_start(out=wqkT[:, D:2 * D], in_=wk_in[h]), 16)
                blk.gpsimd(f)
                t_wqk[h] = s_wqk.tok

                qk_banks = {}
                for step in range(9):
                    bank = next_sc()

                    def f(t, h=h, step=step, bank=bank):
                        if step == 0:
                            wt(t, t_wqk[h])
                            wt(t, t_xnfm)
                        wt(t, bank.freed)
                        if step < 4:
                            q4 = step
                            for dc in range(ND):
                                ins = t.matmul(bank.p[:, :], wqkT[:, dc * 128:(dc + 1) * 128],
                                               xnfm[:, dc * T + q4 * 512:dc * T + (q4 + 1) * 512],
                                               start=(dc == 0), stop=(dc == ND - 1))
                        elif step == 4:
                            for dc in range(ND):
                                ins = t.matmul(bank.p[:, 0:16],
                                               wqkT[:, D + dc * 128:D + (dc + 1) * 128],
                                               g_fmc[:, dc * G:(dc + 1) * G],
                                               start=(dc == 0), stop=(dc == ND - 1))
                        else:
                            q4 = step - 5
                            for dc in range(ND):
                                ins = t.matmul(bank.p[:, :], wqkT[:, D + dc * 128:D + (dc + 1) * 128],
                                               xnfm[:, dc * T + q4 * 512:dc * T + (q4 + 1) * 512],
                                               start=(dc == 0), stop=(dc == ND - 1))
                        bank.used = s_pe.inc(ins)
                    blk.tensor(f)
                    qk_banks[step] = bank

                    def f(dve, h=h, step=step, bank=bank):
                        wt(dve, bank.used)
                        if h >= 1:
                            wt(dve, t_lastscore.get(h - 1))
                        dve.wait_ge(s_dve.h, s_dve.n)
                        if step < 4:
                            ins = dve.tensor_scalar(qb[:, step * 512:(step + 1) * 512],
                                                    bank.p[:, :], QSCALE, None, ALU.mult)
                        elif step == 4:
                            ins = dve.tensor_copy(kb[:, 0:16], bank.p[:, 0:16])
                        else:
                            q4 = step - 5
                            ins = dve.tensor_copy(kb[:, 16 + q4 * 512:16 + (q4 + 1) * 512], bank.p[:, :])
                        bank.freed = s_dve.inc(ins)
                    blk.vector(f)
                t_qkprod[h] = s_pe.tok
                t_qk_evac[h] = s_dve.tok

                if h % 4 == 0:
                    for kt in range(NT + 1):
                        bank = next_sc()

                        def f(t, hg=hg, kt=kt, bank=bank):
                            if kt == 0:
                                wt(t, t_wv[hg])
                                wt(t, t_xnfm)
                            wt(t, bank.freed)
                            if kt < NT:
                                for dc in range(ND):
                                    ins = t.matmul(bank.p[:, :],
                                                   xnfm[:, dc * T + kt * 128:dc * T + (kt + 1) * 128],
                                                   wvT[:, dc * 512:(dc + 1) * 512],
                                                   start=(dc == 0), stop=(dc == ND - 1))
                            else:
                                for dc in range(ND):
                                    ins = t.matmul(bank.p[0:16, :],
                                                   g_fmc[:, dc * G:(dc + 1) * G],
                                                   wvT[:, dc * 512:(dc + 1) * 512],
                                                   start=(dc == 0), stop=(dc == ND - 1))
                            bank.used = s_pe.inc(ins)
                        blk.tensor(f)

                        def f(dve, hg=hg, kt=kt, bank=bank):
                            wt(dve, bank.used)
                            if hg >= 1:
                                wt(dve, t_pv.get((4 * hg - 1, 3)))
                            dve.wait_ge(s_dve.h, s_dve.n)
                            if kt < NT:
                                ins = dve.tensor_copy(v4[:, kt * 512:(kt + 1) * 512], bank.p[:, :])
                            else:
                                ins = dve.tensor_copy(v4[0:16, kt * 512:(kt + 1) * 512], bank.p[0:16, :])
                            bank.freed = s_dve.inc(ins)
                        blk.vector(f)
                    t_vprod[hg] = s_pe.tok
                    t_v_evac[hg] = s_dve.tok

                hh = h % 4
                t_exp = {}
                t_pvmm = {}
                for q4 in range(4):
                    klist = kts(q4) + ["glob"]
                    nk = len(klist)
                    for i, kt in enumerate(klist):
                        bank = next_sc()

                        def f(t, h=h, q4=q4, i=i, kt=kt, bank=bank):
                            if i == 0 and q4 == 0:
                                wt(t, t_qk_evac[h])
                                wt(t, t_v_evac[h // 4])
                            wt(t, bank.freed)
                            if kt == "glob":
                                ins = t.matmul(bank.p[0:16, :], kb[:, 0:16],
                                               qb[:, q4 * 512:(q4 + 1) * 512],
                                               start=True, stop=True)
                            else:
                                t.matmul(bank.p[:, :], kb[:, 16 + kt * 128:16 + (kt + 1) * 128],
                                         qb[:, q4 * 512:(q4 + 1) * 512], start=True, stop=False)
                                v = kt - 4 * q4 + 2
                                ins = t.matmul(bank.p[:, :], ident[:, :],
                                               masks[:, v * 512:(v + 1) * 512],
                                               start=False, stop=True)
                            bank.used = s_pe.inc(ins)
                            t_lastscore[h] = bank.used
                        blk.tensor(f)

                        _t_used = bank.used
                        _t_pb = (t_pvmm.get((h, q4, i - 4)) if i >= 4 else None) or \
                            t_pv.get((h, q4 - 1)) or t_pv.get((h - 1, 3))

                        def f(act, i=i, kt=kt, bank=bank, _t_used=_t_used, _t_pb=_t_pb):
                            wt(act, _t_used)
                            wt(act, _t_pb)
                            if kt == "glob":
                                return act.activation(p_g[:, :], bank.p[0:16, :], AF.Exp)
                            return act.activation(p_sb[:, (i % 4) * 512:(i % 4 + 1) * 512],
                                                  bank.p[:, :], AF.Exp)
                        bank.freed = da_emit(f)
                        t_exp[(h, q4, i)] = bank.freed

                        if i >= 1:
                            emit_pv(h, q4, i - 1, klist, hh, t_exp, t_pvmm, t_pv)
                    emit_pv(h, q4, nk - 1, klist, hh, t_exp, t_pvmm, t_pv)
                    t_pv[(h, q4)] = s_pe.tok
                    while pending_tail:
                        pending_tail.pop(0)()

                    def f(dve, h=h, q4=q4):
                        sl = slice(q4 * 512, (q4 + 1) * 512)
                        wt(dve, t_pv[(h, q4)])
                        dve.wait_ge(s_dve.h, s_dve.n)
                        s_dve.inc(dve.tensor_copy(rowA[:, sl], b_sr.p[0:1, :]))
                    blk.vector(f)
                    b_sr.freed = s_dve.tok

                    def f(dve, h=h, q4=q4):
                        sl = slice(q4 * 512, (q4 + 1) * 512)
                        dve.wait_ge(s_dve.h, s_dve.n)
                        s_dve.inc(dve.reciprocal(rowB[:, sl], rowA[:, sl]))
                        dve.wait_ge(s_dve.h, s_dve.n)
                        s_dve.inc(dve.tensor_copy(rowH[:, sl], rowB[:, sl]))
                        dve.wait_ge(s_dve.h, s_dve.n)
                        s_dve.inc(dve.tensor_tensor(rowC[:, sl], rowB[:, sl], rowH[:, sl],
                                                    ALU.subtract))
                        dve.wait_ge(s_dve.h, s_dve.n)
                        s_dve.inc(dve.tensor_copy(rowL[:, sl], rowC[:, sl]))
                    blk.vector(f)
                    t_rhl4 = s_dve.tok

                    def tail_fin(h=h, q4=q4, t_rhl4=t_rhl4):
                        def f(t):
                            wt(t, t_rhl4)
                            wt(t, b_rb.freed)
                            t.matmul(b_rb.p[:, :], ones_row[:, :], rowH[:, q4 * 512:(q4 + 1) * 512],
                                     start=True, stop=False)
                            ins = t.matmul(b_rb.p[:, :], ones_row[:, :], rowL[:, q4 * 512:(q4 + 1) * 512],
                                           start=False, stop=True)
                            b_rb.used = s_pe.inc(ins)
                        blk.tensor(f)

                        def f(dve):
                            wt(dve, b_rb.used)
                            wt(dve, t_ao.get((h, q4 - 1)) or t_ao.get((h - 1, 3)))
                            dve.wait_ge(s_dve.h, s_dve.n)
                            b_rb.freed = s_dve.inc(dve.tensor_copy(rb_sb[:, :], b_rb.p[:, :]))
                        blk.vector(f)
                        t_rb = s_dve.tok

                        def f(dve, t_rb=t_rb):
                            wt(dve, t_rb)
                            idx = (h * 4 + q4) % 2
                            wt(dve, t_ao_st.get((h * 4 + q4 - 2)))
                            ins = dve.tensor_tensor(aostg[:, idx * 512:(idx + 1) * 512],
                                                    b_pv[q4 % 2].p[:, :], rb_sb[:, :], ALU.mult)
                            s_dve.inc(ins)
                            b_pv[q4 % 2].freed = s_dve.tok
                        blk.vector(f)
                        t_ao[(h, q4)] = s_dve.tok

                        def f(sync, t_aoe=s_dve.tok):
                            wt(sync, t_aoe)
                            idx = (h * 4 + q4) % 2
                            dst = ao_dram[4 * q4:4 * q4 + 4, h].rearrange("j p k -> p j k")
                            srcv = aostg[:, idx * 512:(idx + 1) * 512].rearrange(
                                "p (j k) -> p j k", j=4)
                            s_ao[idx].inc(sync.dma_start(out=dst, in_=srcv), 16)
                        blk.sync(f)
                        t_ao_st[h * 4 + q4] = s_ao[(h * 4 + q4) % 2].tok

                    pending_tail.append(tail_fin)

        while pending_tail:
            pending_tail.pop(0)()
        barrier()
        pp.close()
        p2i.close()
        p1s.close()
        p2s.close()

        p3s = ExitStack()
        v1h = sba(p3s, "v1ht", [128, 8 * D], F32)
        woT = sba(p3s, "woTt", [128, 2 * ND * 1024], BF16)
        aot = sba(p3s, "aott", [128, 2 * D], BF16)
        xres = sba(p3s, "xrest", [128, 2 * D], F32)
        p3p = ExitStack()
        b_wo = [Bank(psa(p3p, "p_wo0", [128, 512])), Bank(psa(p3p, "p_wo1", [128, 512]))]

        t_spill_half = {}
        t_wold = {}
        t_womm = {}
        t_woev = {}
        t_aotld = {}
        t_ln2 = {}
        t_spill = {}
        t_xres = {}

        def f(pool):
            for sl in range(4):
                base = sl * ND * 512
                for ch in range(4):
                    c0 = ch * ND * 128
                    s_wos[sl].inc(pool.dma_start(
                        out=woT[:, base + c0:base + c0 + ND * 128],
                        in_=wo_in[sl][:, c0:c0 + ND * 128]), 16)
        blk.gpsimd(f)
        t_wosl = [s_wos[sl].tok for sl in range(4)]

        for th in range(2):
            for eh in range(2):

                for ti in range(8):
                    tt = th * 8 + ti
                    abuf = ti % 2

                    def f(sync, th=th, eh=eh, ti=ti, tt=tt, abuf=abuf):
                        prev = t_womm.get((th, eh, ti - 2)) or t_womm.get((th, eh - 1, 6 + ti)) \
                            or t_womm.get((th - 1, 1, 6 + ti))
                        wt(sync, prev)
                        src_ = ao_dram[tt].rearrange("h p k -> p h k")
                        dstv = aot[:, abuf * D:(abuf + 1) * D].rearrange(
                            "p (h k) -> p h k", h=NH)
                        s_aot[abuf].inc(sync.dma_start(out=dstv, in_=src_), 16)
                    blk.sync(f)
                    t_aotld[(th, eh, ti)] = s_aot[abuf].tok

                    for e2 in range(2):
                        e4 = 2 * eh + e2
                        bank = b_wo[e2]

                        def f(t, th=th, eh=eh, ti=ti, e2=e2, e4=e4, abuf=abuf, bank=bank):
                            if th == 0 and ti == 0:
                                wt(t, t_wosl[e4])
                            if e2 == 0:
                                wt(t, t_aotld[(th, eh, ti)])
                            wt(t, bank.freed)
                            off = eh * ND * 1024 + e2 * ND * 512
                            for hc in range(NH):
                                ins = t.matmul(bank.p[:, :],
                                               aot[:, abuf * D + hc * 128:abuf * D + (hc + 1) * 128],
                                               woT[:, off + hc * 512:off + (hc + 1) * 512],
                                               start=(hc == 0), stop=(hc == NH - 1))
                            bank.used = s_pe.inc(ins)
                            t_womm[(th, eh, ti)] = bank.used
                        blk.tensor(f)

                        def f(dve, th=th, eh=eh, ti=ti, e2=e2, e4=e4, bank=bank):
                            wt(dve, bank.used)
                            if eh == 0 and e2 == 0 and th >= 1:
                                wt(dve, t_spill_half.get(th - 1))
                            dve.wait_ge(s_dve.h, s_dve.n)
                            ins = dve.tensor_copy(v1h[:, ti * D + e4 * 512:ti * D + (e4 + 1) * 512],
                                                  bank.p[:, :])
                            bank.freed = s_dve.inc(ins)
                        blk.vector(f)
                        t_woev[(th, eh, ti, e2)] = s_dve.tok
                t_womm[(th, eh)] = s_pe.tok

            for ti in range(8):
                tt = th * 8 + ti
                buf = ti % 2

                def f(sync, th=th, ti=ti, tt=tt, buf=buf):
                    wt(sync, t_ln2.get((th, ti - 2)) or t_ln2.get((th - 1, 6 + ti)))
                    s_xr[buf].inc(sync.dma_start(out=xres[:, buf * D:(buf + 1) * D],
                                                 in_=x_in[tt * 128:(tt + 1) * 128, :]), 16)
                blk.sync(f)
                t_xres[(th, ti)] = s_xr[buf].tok

                def f(dve, th=th, ti=ti, buf=buf):
                    wt(dve, t_xres[(th, ti)])
                    wt(dve, t_woev[(th, 1, ti, 1)])
                    vv = v1h[:, ti * D:(ti + 1) * D]
                    dve.wait_ge(s_dve.h, s_dve.n)
                    s_dve.inc(dve.scalar_tensor_tensor(vv, xres[:, buf * D:(buf + 1) * D],
                                                       2.0, vv, ALU.mult, ALU.add))
                blk.vector(f)
                t_v1 = s_dve.tok

                t_rstd = emit_ln_stats(lambda ti=ti: v1h[:, ti * D:(ti + 1) * D], t_v1)

                def f(dve, th=th, ti=ti, t_rstd=t_rstd):
                    wt(dve, t_rstd)
                    vv = v1h[:, ti * D:(ti + 1) * D]
                    dve.wait_ge(s_dve.h, s_dve.n)
                    s_dve.inc(dve.tensor_scalar(vv, vv, agm[:, 0:1], rstd[:, 0:1],
                                                ALU.subtract, ALU.mult))
                blk.vector(f)
                t_ln2[(th, ti)] = s_dve.tok

                def f(sync, tt=tt, ti=ti, t_r2=s_dve.tok):
                    wt(sync, t_r2)
                    s_r2s.inc(sync.dma_start(out=r2_dram[tt * 128:(tt + 1) * 128, :],
                                              in_=v1h[:, ti * D:(ti + 1) * D]), 16)
                blk.sync(f)
                t_spill[(th, ti)] = s_r2s.tok
            t_spill_half[th] = s_r2s.tok

        barrier()
        p3p.close()
        p3s.close()

        p4s = ExitStack()
        hmidt = sba(p4s, "hmidt", [128, NHB * 1024], BF16)
        hmid = hmidt

        t_gu = {}
        t_hmid = {}
        t_hmid_all = {}
        t_dpe = {}
        t_ln3 = {}
        t_outst = {}
        t_v2ld = {}
        t_v2spst = {}
        t_cshl = {}
        t_hcshl = {}
        t_trall = {}
        t_trtile = {}
        t_r2rd = {}
        reb = {}

        def emit_rebuild_loads(p):
            pxs = ExitStack()
            xn2h = sba(pxs, "xn2ht%d" % p, [128, ND * 1024], BF16, is_transpose_dest=True)
            prb = ExitStack()
            n2p = sba(prb, "n2pt%d" % p, [128, 4 * D], BF16)
            reb[p] = dict(pxs=pxs, prb=prb, xn2h=xn2h, n2p=n2p, t_n2p={})

            def f(pool, p=p, n2p=n2p):
                if p > 0:
                    wt(pool, t_cshl[p - 1])
                for ti in range(4):
                    tt = p * 8 + ti
                    s_n2l[ti].inc(pool.dma_start(out=n2p[:, ti * D:(ti + 1) * D],
                                                 in_=r2_dram[tt * 128:(tt + 1) * 128, :]), 16)
            blk.gpsimd(f)
            for ti in range(4):
                reb[p]["t_n2p"][ti] = s_n2l[ti].tok

        def emit_late_load(p, ti):
            st = reb[p]

            def f(pool, p=p, ti=ti, n2p=st["n2p"]):
                wt(pool, t_trtile[(p, ti - 4)])
                tt = p * 8 + ti
                s_n2l[ti % 4].inc(pool.dma_start(out=n2p[:, (ti % 4) * D:(ti % 4 + 1) * D],
                                                 in_=r2_dram[tt * 128:(tt + 1) * 128, :]), 16)
            blk.gpsimd(f)
            st["t_n2p"][ti] = s_n2l[ti % 4].tok

        def emit_transpose_tile(p, ti):
            st = reb[p]

            def f(sync, p=p, ti=ti, xn2h=st["xn2h"], n2p=st["n2p"], t_ld=st["t_n2p"][ti]):
                wt(sync, t_ld)
                if ti == 0 and p > 0:
                    wt(sync, t_gu[(p - 1, NHB - 1)])
                    wt(sync, t_hmid[(p - 1, NHB - 1, 1)])
                for c in range(ND):
                    ins = sync.dma_start_transpose(
                        out=xn2h[:, c * 1024 + ti * 128:c * 1024 + (ti + 1) * 128],
                        in_=n2p[:, (ti % 4) * D + c * 128:(ti % 4) * D + (c + 1) * 128])
                    s_tr[ti].inc(ins, 16)
            blk.sync(f)
            t_trtile[(p, ti)] = s_tr[ti].tok

        def emit_colsum(p):
            xn2h = reb[p]["xn2h"]
            t_csev = {}
            for tc in range(2):
                def f(t, p=p, tc=tc, xn2h=xn2h):
                    if tc == 0:
                        for tok in t_trall[p]:
                            wt(t, tok)
                    wt(t, b_cs.freed)
                    for dc in range(ND):
                        ins = t.matmul(b_cs.p[0:1, :], ones_col[:, :],
                                       xn2h[:, dc * 1024 + tc * 512:dc * 1024 + (tc + 1) * 512],
                                       start=(dc == 0), stop=(dc == ND - 1))
                    b_cs.used = s_pe.inc(ins)
                blk.tensor(f)

                def f(dve, tc=tc):
                    wt(dve, b_cs.used)
                    dve.wait_ge(s_dve.h, s_dve.n)
                    b_cs.freed = s_dve.inc(dve.tensor_copy(rowA[:, tc * 512:(tc + 1) * 512],
                                                           b_cs.p[0:1, :]))
                blk.vector(f)
                t_csev[tc] = s_dve.tok

            def f(dve, p=p):
                wt(dve, t_csev[1])
                if p > 0:
                    wt(dve, t_dpe[(p - 1, 3)])
                dve.wait_ge(s_dve.h, s_dve.n)
                s_dve.inc(dve.tensor_copy(rowH[:, 0:1024], rowA[:, 0:1024]))
                dve.wait_ge(s_dve.h, s_dve.n)
                s_dve.inc(dve.tensor_tensor(rowC[:, 0:1024], rowA[:, 0:1024],
                                            rowH[:, 0:1024], ALU.subtract))
                dve.wait_ge(s_dve.h, s_dve.n)
                s_dve.inc(dve.tensor_copy(rowL[:, 0:1024], rowC[:, 0:1024]))
            blk.vector(f)
            t_cshl[p] = s_dve.tok

        def emit_gate_up(p, ln3_hook=None):
            xn2h = reb[p]["xn2h"]
            pgu = ExitStack()
            wgT = sba(pgu, "wgTt%d" % p, [128, 2 * D], BF16)
            wuT = sba(pgu, "wuTt%d" % p, [128, 2 * D], BF16)
            gus = sba(pgu, "gust%d" % p, [128, 1024], F32)
            t_wgld = {}
            t_gact = {}
            for hb in range(NHB):
                j = hb % 2

                def f(pool, p=p, hb=hb, j=j, wgT=wgT, wuT=wuT):
                    prev = t_gu.get((p, hb - 2)) or t_gu.get((p - 1, NHB - 2 + hb))
                    wt(pool, prev)
                    if hb < 2:
                        for tok in t_trall[p]:
                            wt(pool, tok)
                    s_wg[j].inc(pool.dma_start(out=wgT[:, j * D:(j + 1) * D], in_=wg_in[hb]), 16)
                    s_wg[j].inc(pool.dma_start(out=wuT[:, j * D:(j + 1) * D], in_=wu_in[hb]), 16)
                blk.gpsimd(f)
                t_wgld[hb] = s_wg[j].tok

                for tc in range(2):
                    for gu in range(2):
                        bank = b_gu[tc * 2 + gu]

                        def f(t, p=p, hb=hb, j=j, tc=tc, gu=gu, bank=bank,
                              wgT=wgT, wuT=wuT, xn2h=xn2h):
                            if tc == 0 and gu == 0:
                                wt(t, t_wgld[hb])
                                wt(t, t_cshl[p])
                            wt(t, bank.freed)
                            wsl = (wgT if gu == 0 else wuT)
                            for dc in range(ND):
                                t.matmul(bank.p[:, :], wsl[:, j * D + dc * 128:j * D + (dc + 1) * 128],
                                         xn2h[:, dc * 1024 + tc * 512:dc * 1024 + (tc + 1) * 512],
                                         start=(dc == 0), stop=False)
                            t.matmul(bank.p[:, :], neg128[:, 0:128],
                                     rowH[:, tc * 512:(tc + 1) * 512], start=False, stop=False)
                            ins = t.matmul(bank.p[:, :], neg128[:, 0:128],
                                           rowL[:, tc * 512:(tc + 1) * 512], start=False, stop=True)
                            bank.used = s_pe.inc(ins)
                        blk.tensor(f)
                    t_gu[(p, hb)] = s_pe.tok

                for tc in range(2):
                    bg = b_gu[tc * 2]
                    bu = b_gu[tc * 2 + 1]

                    _prev = t_hmid.get((p, hb, tc - 1)) if tc == 1 else \
                        (t_hmid.get((p, hb - 1, 1)) or t_hmid.get((p - 1, NHB - 1, 1)))

                    def f(act, hb=hb, bg=bg, gus=gus, _t_used=bg.used, _prev=_prev):
                        wt(act, _t_used)
                        wt(act, _prev)
                        return act.activation(gus[:, 0:512], bg.p[:, :], AF.Silu,
                                              scale=sgt[:, hb:hb + 1])
                    bg.freed = da_emit(f)
                    t_gact[(hb, tc)] = bg.freed

                    def f(dve, p=p, hb=hb, tc=tc, bu=bu, gus=gus):
                        wt(dve, bu.used)
                        if hb == 0 and tc == 0:
                            wt(dve, t_dpe.get((p - 1, 3)))
                        dve.wait_ge(s_dve.h, s_dve.n)
                        ins = dve.tensor_scalar(gus[:, 512:1024], bu.p[:, :],
                                                sut[:, hb:hb + 1], None, ALU.mult)
                        bu.freed = s_dve.inc(ins)
                        wt(dve, t_gact[(hb, tc)])
                        dve.wait_ge(s_dve.h, s_dve.n)
                        ins = dve.tensor_tensor(
                            hmid[:, hb * 1024 + tc * 512:hb * 1024 + (tc + 1) * 512],
                            gus[:, 0:512], gus[:, 512:1024], ALU.mult)
                        s_dve.inc(ins)
                    blk.vector(f)
                    t_hmid[(p, hb, tc)] = s_dve.tok

                if ln3_hook is not None and hb >= 2 and hb <= 30 and hb % 4 == 2:
                    ln3_hook((hb - 2) // 4)
            t_hmid_all[p] = s_dve.tok
            pgu.close()

        def emit_hcolsum(p):
            t_hcs_ev = {}
            for tc in range(2):
                def f(t, p=p, tc=tc):
                    wt(t, t_hmid_all[p])
                    wt(t, b_cs.freed)
                    for hb in range(NHB):
                        ins = t.matmul(b_cs.p[0:1, :], ones_col[:, :],
                                       hmid[:, hb * 1024 + tc * 512:hb * 1024 + (tc + 1) * 512],
                                       start=(hb == 0), stop=(hb == NHB - 1))
                    b_cs.used = s_pe.inc(ins)
                blk.tensor(f)

                def f(dve, tc=tc):
                    wt(dve, b_cs.used)
                    dve.wait_ge(s_dve.h, s_dve.n)
                    b_cs.freed = s_dve.inc(dve.tensor_copy(rowA[:, tc * 512:(tc + 1) * 512],
                                                           b_cs.p[0:1, :]))
                blk.vector(f)
                t_hcs_ev[tc] = s_dve.tok

            def f(dve):
                wt(dve, t_hcs_ev[1])
                dve.wait_ge(s_dve.h, s_dve.n)
                s_dve.inc(dve.tensor_copy(rowH[:, 0:1024], rowA[:, 0:1024]))
                dve.wait_ge(s_dve.h, s_dve.n)
                s_dve.inc(dve.tensor_tensor(rowC[:, 0:1024], rowA[:, 0:1024],
                                            rowH[:, 0:1024], ALU.subtract))
                dve.wait_ge(s_dve.h, s_dve.n)
                s_dve.inc(dve.tensor_copy(rowL[:, 0:1024], rowC[:, 0:1024]))
            blk.vector(f)
            t_hcshl[p] = s_dve.tok

        def emit_down(p, wdT, v2sp, tr_hook=None, ld_hook=None):
            t_wdld = {}
            for e4 in range(4):
                for hb in range(NHB):
                    j = hb % 2

                    def f(pool, p=p, e4=e4, hb=hb, j=j, wdT=wdT):
                        prev = t_dpe.get((p, e4, hb - 2)) or t_dpe.get((p, e4 - 1, NHB - 2 + hb)) \
                            or t_dpe.get((p - 1, 3, NHB - 2 + hb))
                        wt(pool, prev)
                        if e4 == 0 and hb < 2:
                            wt(pool, t_gu[(p, NHB - 1)])
                            wt(pool, t_hmid[(p, NHB - 1, 1)])
                            if p > 0:
                                wt(pool, t_outst[(p - 1, 7)])
                        s_wd[j].inc(pool.dma_start(out=wdT[:, j * 512:(j + 1) * 512],
                                                   in_=wd_in[e4, hb * 128:(hb + 1) * 128, :]), 16)
                    blk.gpsimd(f)
                    t_wdld[(e4, hb)] = s_wd[j].tok
                    if ld_hook is not None and e4 == 0 and hb >= 8 and hb % 4 == 0 and hb < 24:
                        ld_hook(4 + (hb - 8) // 4)

                    def f(t, p=p, e4=e4, hb=hb, j=j, wdT=wdT):
                        wt(t, t_wdld[(e4, hb)])
                        if hb == 0:
                            wt(t, t_hcshl[p])
                            for ti in range(8):
                                wt(t, pd_banks[ti].freed)
                        for ti in range(8):
                            ins = t.matmul(pd_banks[ti].p[:, :],
                                           hmid[:, hb * 1024 + ti * 128:hb * 1024 + (ti + 1) * 128],
                                           wdT[:, j * 512:(j + 1) * 512],
                                           start=(hb == 0), stop=False)
                        if hb == NHB - 1:
                            for ti in range(8):
                                t.matmul(pd_banks[ti].p[:, :],
                                         rowH[:, ti * 128:(ti + 1) * 128], neg128[:, :],
                                         start=False, stop=False)
                                ins = t.matmul(pd_banks[ti].p[:, :],
                                               rowL[:, ti * 128:(ti + 1) * 128], neg128[:, :],
                                               start=False, stop=True)
                                pd_banks[ti].used = s_pe.inc(ins)
                        else:
                            s_pe.inc(ins)
                        t_dpe[(p, e4, hb)] = s_pe.tok
                    blk.tensor(f)
                t_dpe[(p, e4)] = s_pe.tok

                for ti in range(8):
                    bank = pd_banks[ti]

                    def f(dve, p=p, e4=e4, ti=ti, bank=bank, v2sp=v2sp):
                        wt(dve, bank.used)
                        dve.wait_ge(s_dve.h, s_dve.n)
                        wt(dve, s_v2s[ti % 2].tok)
                        dst = v2sp[:, (ti % 2) * 512:(ti % 2 + 1) * 512]
                        ins = dve.tensor_tensor(dst, bank.p[:, :],
                                                sd_bc[:, e4 * 512:(e4 + 1) * 512], ALU.mult)
                        bank.freed = s_dve.inc(ins)
                    blk.vector(f)

                    def f(sync, p=p, e4=e4, ti=ti, t_e=s_dve.tok, v2sp=v2sp):
                        wt(sync, t_e)
                        s_v2s[ti % 2].inc(sync.dma_start(
                            out=v2_dram[p, e4, ti],
                            in_=v2sp[:, (ti % 2) * 512:(ti % 2 + 1) * 512]), 16)
                    blk.sync(f)
                    t_v2spst[(p, e4, ti)] = s_v2s[ti % 2].tok
                    if tr_hook is not None and e4 == 1 and ti % 2 == 1:
                        tr_hook(4 + ti // 2)

        def emit_ln3_tile(p, ti, v2a, r2tl, nsl):
            tt = p * 8 + ti
            buf = ti % nsl

            def f(sync, p=p, ti=ti, tt=tt, buf=buf, v2a=v2a, r2tl=r2tl):
                wt(sync, t_v2spst[(p, 3, ti)])
                wt(sync, t_outst.get((p, ti - 1)) or t_outst.get((p - 1, 7)))
                src = v2_dram[p, :, ti].rearrange("e p k -> p e k")
                dstv = v2a[:, :].rearrange("p (e k) -> p e k", e=4)
                s_v2l.inc(sync.dma_start(out=dstv, in_=src), 16)
                wt(sync, t_r2rd.get((p, ti - nsl)) or (t_ln3.get((p - 1, 7)) if p > 0 else None))
                s_r2l[buf].inc(sync.dma_start(out=r2tl[:, buf * D:(buf + 1) * D],
                                              in_=r2_dram[tt * 128:(tt + 1) * 128, :]), 16)
            blk.sync(f)
            t_v2ld[(p, ti)] = (s_v2l, s_v2l.n)
            t_r2ld3 = s_r2l[buf].tok

            def f(dve, p=p, ti=ti, buf=buf, t_r2ld3=t_r2ld3, v2a=v2a, r2tl=r2tl):
                wt(dve, t_v2ld[(p, ti)])
                wt(dve, t_r2ld3)
                dve.wait_ge(s_dve.h, s_dve.n)
                s_dve.inc(dve.tensor_tensor(v2a[:, :], v2a[:, :],
                                            r2tl[:, buf * D:(buf + 1) * D], ALU.add))
            blk.vector(f)
            t_v2 = s_dve.tok
            t_r2rd[(p, ti)] = t_v2

            t_rstd = emit_ln_stats(lambda v2a=v2a: v2a[:, :], t_v2)

            def f(dve, p=p, ti=ti, t_rstd=t_rstd, v2a=v2a):
                wt(dve, t_rstd)
                dve.wait_ge(s_dve.h, s_dve.n)
                s_dve.inc(dve.tensor_scalar(v2a[:, :], v2a[:, :], agm[:, 0:1], rstd[:, 0:1],
                                            ALU.subtract, ALU.mult))
            blk.vector(f)
            t_ln3[(p, ti)] = s_dve.tok

            def f(sync, tt=tt, t_o=s_dve.tok, v2a=v2a):
                wt(sync, t_o)
                s_out.inc(sync.dma_start(out=out_dram[tt * 128:(tt + 1) * 128, :],
                                          in_=v2a[:, :]), 16)
            blk.sync(f)
            t_outst[(p, ti)] = s_out.tok

        emit_rebuild_loads(0)
        ptr = ExitStack()
        trb = [Bank(ptr.enter_context(nc.psum_tensor("p_trz%d" % i, [128, 1024], BF16)))
               for i in range(6)]
        for ti in range(8):
            if ti >= 4:
                emit_late_load(0, ti)
            st0 = reb[0]
            for c in range(ND):
                bank = trb[(ti * ND + c) % 6]

                def f(t, ti=ti, c=c, bank=bank, n2p=st0["n2p"], t_ld=st0["t_n2p"][ti]):
                    if c == 0:
                        wt(t, t_ld)
                    wt(t, bank.freed)
                    ins = t.transpose(bank.p[:, 0:128],
                                      n2p[:, (ti % 4) * D + c * 128:(ti % 4) * D + (c + 1) * 128],
                                      ident[:, :])
                    bank.used = s_pe.inc(ins)
                blk.tensor(f)

                def f(dve, ti=ti, c=c, bank=bank, xn2h=st0["xn2h"]):
                    wt(dve, bank.used)
                    dve.wait_ge(s_dve.h, s_dve.n)
                    ins = dve.tensor_copy(xn2h[:, c * 1024 + ti * 128:c * 1024 + (ti + 1) * 128],
                                          bank.p[:, 0:128])
                    bank.freed = s_dve.inc(ins)
                blk.vector(f)
            t_trtile[(0, ti)] = s_pe.tok
        ptr.close()
        t_trall[0] = [s_pe.tok, s_dve.tok]
        p4p = ExitStack()
        b_gu = [Bank(psa(p4p, "p_gu%d" % i, [128, 512])) for i in range(4)]
        b_cs = Bank(psa(p4p, "p_cs2", [128, 512]))
        b_s2 = [Bank(psa(p4p, "p_s20", [128, 512])), Bank(psa(p4p, "p_s21", [128, 512]))]
        b_x = Bank(psa(p4p, "p_x2", [128, 512]))
        pd_banks = [b_cs] + b_gu + b_s2 + [b_x]
        emit_colsum(0)
        reb[0]["prb"].close()
        emit_gate_up(0)
        reb[0]["pxs"].close()
        emit_hcolsum(0)

        pdn0 = ExitStack()
        wdT0 = sba(pdn0, "wdTt0", [128, 2 * 512], BF16)
        v2sp0 = sba(pdn0, "v2spt0", [128, 2 * 512], F32)
        v2a0 = sba(pdn0, "v2at0", [128, D], F32)
        r2t0 = sba(pdn0, "r2t0t", [128, D], F32)
        emit_rebuild_loads(1)
        for ti in range(4):
            emit_transpose_tile(1, ti)
        emit_down(0, wdT0, v2sp0, tr_hook=lambda k: emit_transpose_tile(1, k),
                  ld_hook=lambda k: emit_late_load(1, k))
        t_trall[1] = [s_tr[i].tok for i in range(8)]

        emit_colsum(1)
        reb[1]["prb"].close()
        emit_gate_up(1, ln3_hook=lambda ti: emit_ln3_tile(0, ti, v2a0, r2t0, 1))
        reb[1]["pxs"].close()
        pdn0.close()
        emit_hcolsum(1)

        pdn1 = ExitStack()
        wdT1 = sba(pdn1, "wdTt1", [128, 2 * 512], BF16)
        v2sp1 = sba(pdn1, "v2spt1", [128, 2 * 512], F32)
        v2a1 = sba(pdn1, "v2at1", [128, D], F32)
        r2t1 = sba(pdn1, "r2t1t", [128, 2 * D], F32)
        emit_down(1, wdT1, v2sp1)
        for ti in range(8):
            emit_ln3_tile(1, ti, v2a1, r2t1, 2)
        pdn1.close()
        p4p.close()
        p4s.close()

        flush_act()

        def f(sync):
            sync.wait_ge(s_out.h, s_out.n)
        blk.sync(f)

    outer.close()
    return nc


def emit_pv(h, q4, i, klist, hh, t_exp, t_pvmm, t_pv):
    blk = emit_pv.blk
    s_pe = emit_pv.s_pe
    b_pv = emit_pv.b_pv
    b_sr = emit_pv.b_sr
    v4 = emit_pv.v4
    p_sb = emit_pv.p_sb
    p_g = emit_pv.p_g
    ones_col = emit_pv.ones_col
    kt = klist[i]
    nk = len(klist)
    bpv = b_pv[q4 % 2]

    def f(t):
        wt(t, t_exp[(h, q4, i)])
        if i == 0:
            wt(t, bpv.freed)
            wt(t, b_sr.freed)
        first = (i == 0)
        last = (i == nk - 1)
        if kt == "glob":
            t.matmul(bpv.p[:, :], v4[0:16, 16 * 512 + hh * 128:16 * 512 + (hh + 1) * 128],
                     p_g[:, :], start=first, stop=last)
            ins = t.matmul(b_sr.p[0:1, :], ones_col[0:16, :], p_g[:, :],
                           start=first, stop=last)
        else:
            t.matmul(bpv.p[:, :], v4[:, kt * 512 + hh * 128:kt * 512 + (hh + 1) * 128],
                     p_sb[:, (i % 4) * 512:(i % 4 + 1) * 512], start=first, stop=last)
            ins = t.matmul(b_sr.p[0:1, :], ones_col[:, :],
                           p_sb[:, (i % 4) * 512:(i % 4 + 1) * 512], start=first, stop=last)
        tok = s_pe.inc(ins)
        t_pvmm[(h, q4, i)] = tok
        if last:
            bpv.used = tok
            b_sr.used = tok
    blk.tensor(f)


def host_prepare(L0_attn_x, L0_attn_global_tokens, L0_attn_ln_scale, L0_attn_ln_bias,
                 L0_attn_Wq, L0_attn_Wk, L0_attn_Wv, L0_attn_Wo,
                 L0_ffn_wq_gate, L0_ffn_s_gate, L0_ffn_wq_up, L0_ffn_s_up,
                 L0_ffn_wq_down, L0_ffn_s_down):
    ac = np.ascontiguousarray
    x = np.asarray(L0_attn_x, dtype=np.float32)
    g = ac(np.asarray(L0_attn_global_tokens, dtype=np.float32).reshape(G, D))
    lnw = ac(np.asarray(L0_attn_ln_scale, np.float32).reshape(ND, 128).T)
    lnb = ac(np.asarray(L0_attn_ln_bias, np.float32).reshape(ND, 128).T)
    def warr(w, nslice, csz):
        return ac(np.asarray(w, np.float32).T.reshape(ND, 128, nslice, csz)
                  .transpose(2, 1, 0, 3).reshape(nslice, 128, ND * csz))

    def warr_i(w, nslice, csz):
        return ac(np.asarray(w, np.int32).T.reshape(ND, 128, nslice, csz)
                  .transpose(2, 1, 0, 3).reshape(nslice, 128, ND * csz).astype(np.int32))

    wq = warr(L0_attn_Wq, NH, HD)
    wk = warr(L0_attn_Wk, NH, HD)
    wv = warr(L0_attn_Wv, 4, 512)
    wo = warr(L0_attn_Wo, 4, 512)
    wg = warr_i(L0_ffn_wq_gate, NHB, 128)
    wu = warr_i(L0_ffn_wq_up, NHB, 128)
    wd = ac(np.asarray(L0_ffn_wq_down, np.int32).T.reshape(HID, 4, 512).transpose(1, 0, 2))
    sg = ac(np.asarray(L0_ffn_s_gate, np.float32).reshape(NHB, 128).T)
    su = ac(np.asarray(L0_ffn_s_up, np.float32).reshape(NHB, 128).T)
    sd = ac(np.asarray(L0_ffn_s_down, np.float32).reshape(1, D))

    ident = np.eye(128, dtype=bfdt)
    ones_col = np.ones((128, 1), dtype=bfdt)
    ones_row = np.ones((1, 128), dtype=bfdt)
    neg128 = np.full((1, 512), -128.0, dtype=bfdt)
    pidx = np.arange(128)[:, None]
    fidx = np.arange(512)[None, :]
    m6 = np.empty((128, 6 * 512), dtype=bfdt)
    for iv in range(6):
        v = iv - 2
        diff = fidx - v * 128 - pidx
        valid = (diff >= 0) & (diff <= WIN - 1)
        m6[:, iv * 512:(iv + 1) * 512] = np.where(valid, 0.0, NEG).astype(bfdt)

    shared = dict(g=g, lnw=lnw, lnb=lnb, wq=wq, wk=wk, wv=wv, wo=wo,
                  wg=wg, wu=wu, wd=wd, sgv=sg, suv=su, sdv=sd,
                  ident=ident, mask6=m6, ones_col=ones_col, ones_row=ones_row,
                  neg128=neg128)
    in_maps = []
    for core in range(8):
        m = dict(shared)
        m["x"] = ac(x[core])
        in_maps.append(m)
    return in_maps


_NC_CACHE = {}


def kernel(**inputs):
    in_maps = host_prepare(**inputs)
    if "nc" not in _NC_CACHE:
        _NC_CACHE["nc"] = build_nc()
    nc = _NC_CACHE["nc"]
    res = run_bass_kernel_spmd(nc, in_maps, core_ids=list(range(8)))
    out = np.stack([np.asarray(res.results[i]["out"]) for i in range(8)], axis=0)
    return out.astype(np.float32)
